# version 28 (final)
# baseline (speedup 1.0000x reference)
import contextlib
import numpy as np
import concourse.bass as bass
import concourse.mybir as mybir
from concourse.bass_utils import run_bass_kernel_spmd

F32 = mybir.dt.float32
BF16 = mybir.dt.bfloat16
U32 = mybir.dt.uint32
AF = mybir.ActivationFunctionType
ALU = mybir.AluOpType

REAL = dict(
    B=2, S=4096, D=4096, HQ=32, HKV=8, HD=128, WIN=1024,
    OWN=1024, RANKS=(8, 16, 32, 64), EPS=1e-5, THETA=10000.0,
)


def derived(cfg):
    c = dict(cfg)
    c["G"] = c["HQ"] // c["HKV"]
    c["DC"] = c["D"] // 128
    c["EXT"] = c["OWN"] + c["WIN"]
    c["KT"] = c["EXT"] // 128
    c["NPASS"] = 2
    c["PT"] = c["OWN"] // c["NPASS"]
    c["QS"] = c["PT"] // 128
    c["NSUB"] = c["OWN"] // 128
    c["KW"] = c["WIN"] + 128
    c["OB"] = c["D"] // 512
    c["MAXR"] = max(c["RANKS"])
    c["NG"] = len(c["RANKS"])
    c["SM"] = c["HD"] ** -0.5
    c["NCORE_PER_B"] = c["S"] // c["OWN"]
    c["GRP"] = cfg.get("GRP_OVR") or min(512, c["EXT"])
    c["HGRP"] = c["EXT"] // c["GRP"]
    return c


class Res:
    __slots__ = ("w", "r", "dmasem")

    def __init__(self):
        self.w = None
        self.r = {}
        self.dmasem = None

    def take(self, other):
        toks = list(other.r.items())
        if other.w is not None:
            toks.append(other.w)
        for q, v in toks:
            if self.r.get(q, 0) < v:
                self.r[q] = v


class Prog:
    QUEUES = ("pe", "act", "dve", "gdma", "sdma", "pool")
    DMAQ = ("gdma", "sdma")

    @staticmethod
    def stream(q):
        return "gdma" if q == "pool" else q

    def __init__(self, nc, es):
        self.nc = nc
        self.es = es
        self.semh = {}
        self.cnt = {}
        for q in self.QUEUES:
            if q not in self.DMAQ:
                self.semh[q] = es.enter_context(nc.semaphore("s_" + q))
                self.cnt[q] = 0
        self.known = {self.stream(q): {} for q in self.QUEUES}
        self.items = {self.stream(q): [] for q in self.QUEUES}
        self._nds = 0

    def new_dmasem(self):
        name = "ds%d" % self._nds
        self._nds += 1
        self.semh[name] = self.es.enter_context(self.nc.semaphore(name))
        self.cnt[name] = 0
        return name

    def _wait(self, q, tok):
        if tok is None:
            return
        q = self.stream(q)
        pq, v = tok
        if v <= 0 or self.known[q].get(pq, 0) >= v:
            return
        sem = self.semh[pq]
        self.items[q].append(lambda e, sem=sem, v=v: e.wait_ge(sem, v))
        self.known[q][pq] = v

    def do(self, q, fns, reads=(), writes=(), extra_waits=()):
        if not isinstance(fns, (list, tuple)):
            fns = [fns]
        is_dma = q in self.DMAQ
        if is_dma:
            res0 = writes[0]
            if res0.dmasem is None:
                res0.dmasem = self.new_dmasem()
            sname = res0.dmasem
            step = 16
        else:
            sname = q
            step = 1
        for t in extra_waits:
            self._wait(q, t)
        for res in reads:
            self._wait(q, res.w)
        for res in writes:
            if res.w is not None and not (is_dma and res.w[0] == sname):
                self._wait(q, res.w)
            for pq, v in res.r.items():
                self._wait(q, (pq, v))
        sem = self.semh[sname]
        sq = self.stream(q)
        for fn in fns[:-1]:
            self.items[sq].append(lambda e, fn=fn: fn(e))
        last = fns[-1]

        def wrapped(e, fn=last, sem=sem, step=step):
            fn(e).then_inc(sem, step)
        self.items[sq].append(wrapped)
        self.cnt[sname] += step
        tok = (sname, self.cnt[sname])
        for res in reads:
            if res.r.get(q, 0) < self.cnt.get(q, 0) and not is_dma:
                pass
        if not is_dma:
            for res in reads:
                if res.r.get(sname, 0) < tok[1]:
                    res.r[sname] = tok[1]
        else:
            for res in reads:
                if res.r.get(sname, 0) < tok[1]:
                    res.r[sname] = tok[1]
        for res in writes:
            res.w = tok
            res.r = {}
        return tok

    def raw(self, q, fn):
        self.items[q].append(fn)


class Arena:
    def __init__(self, t):
        self.t = t

    def bf(self, off, n, p0=0, p1=128):
        return self.t[p0:p1, off:off + n]

    def f32(self, off, n, p0=0, p1=128):
        assert off % 2 == 0
        return self.t[:, :].bitcast(F32)[p0:p1, off // 2: off // 2 + n]


def build_program(cfg):
    c = derived(cfg)
    D, DC, HQ, HKV, G = c["D"], c["DC"], c["HQ"], c["HKV"], c["G"]
    OWN, EXT, WIN, KW = c["OWN"], c["EXT"], c["WIN"], c["KW"]
    NPASS, PT, QS, KT, OB = c["NPASS"], c["PT"], c["QS"], c["KT"], c["OB"]
    NSUB, MAXR, NG = c["NSUB"], c["MAXR"], c["NG"]
    GRP, HGRP = c["GRP"], c["HGRP"]
    HC = DC // 2

    nc = bass.Bass("TRN2", target_bir_lowering=False)

    def din(name, shape):
        return nc.dram_tensor(name, list(shape), F32, kind="ExternalInput").ap()

    xo = din("xo", (OWN, D))
    xh = din("xh", (WIN, D))
    wq_l = din("wq_l", (HQ, 128, DC * 128))
    wk_l = din("wk_l", (HKV, 128, DC * 128))
    wv_l = din("wv_l", (HKV, 128, DC * 128))
    wo_r = din("wo_r", (OB, HQ, 128, 512))
    W_r = din("W_r", (OB, DC, 128, 512))
    A_l = din("A_l", (128, DC * MAXR))
    B_l = din("B_l", (MAXR, D))
    g_l = din("g_l", (128, DC * NG))
    gb_c = din("gb_c", (NG, 1))
    b_row = din("b_row", (1, D))
    cos_e = din("cos_e", (128, EXT))
    sin_e = din("sin_e", (128, EXT))
    corr_l = din("corr_l", (128, NSUB))
    idnt = din("idnt", (128, 128))
    p128 = din("p128", (128, 128))
    m1t = din("m1t", (128, 128))
    m2t = din("m2t", (128, 128))
    thr = din("thr", (MAXR, 1))
    ones_r = din("ones_r", (1, 512))
    y = nc.dram_tensor("y", [OWN, D], F32, kind="ExternalOutput").ap()

    with contextlib.ExitStack() as es:
        E = es.enter_context
        cols_xg = DC * GRP
        cols_D = HQ * PT
        cols_xfm = DC * PT
        cols_krot = HKV * EXT
        cols_vtm = KT * HKV * 128
        cols_z1 = DC * PT
        B1 = E(nc.sbuf_tensor("B1", [128, max(cols_xg, cols_D, 4 * D)], BF16))
        B2 = E(nc.sbuf_tensor("B2", [128, max(cols_xg, cols_xfm, 4 * D,
                                              DC * MAXR + 3 * D + OWN)], BF16))
        B3 = E(nc.sbuf_tensor("B3", [128, max(cols_krot, cols_z1, 4 * D)], BF16))
        B4 = E(nc.sbuf_tensor("B4", [128, max(cols_vtm, 4 * D)], BF16))
        B5 = E(nc.sbuf_tensor("B5", [128, max(cols_z1, 4 * D, cols_D)], BF16))
        a1, a2, a3, a4, a5 = Arena(B1), Arena(B2), Arena(B3), Arena(B4), Arena(B5)

        c_cos = E(nc.sbuf_tensor("c_cos", [128, EXT], BF16))
        c_sin = E(nc.sbuf_tensor("c_sin", [128, EXT], BF16))
        c_id = E(nc.sbuf_tensor("c_id", [128, 128], BF16))
        c_idf = E(nc.sbuf_tensor("c_idf", [128, 128], F32))
        c_p128 = E(nc.sbuf_tensor("c_p128", [128, 128], BF16))
        c_m1 = E(nc.sbuf_tensor("c_m1", [128, 128], BF16))
        c_m2 = E(nc.sbuf_tensor("c_m2", [128, 128], BF16))
        c_ones = E(nc.sbuf_tensor("c_ones", [1, 512], BF16))
        c_thr = E(nc.sbuf_tensor("c_thr", [MAXR, 1], F32))
        c_corr = E(nc.sbuf_tensor("c_corr", [128, NSUB], F32))
        c_gbc = E(nc.sbuf_tensor("c_gbc", [NG, 1], F32))
        c_g = E(nc.sbuf_tensor("c_g", [128, DC * NG], BF16))

        SCR_N = 14720
        SCR = E(nc.sbuf_tensor("SCR", [128, SCR_N], BF16))
        scr = Arena(SCR)
        SLC_N = DC * 128
        SLC = (0, SLC_N)
        WRS = (2 * SLC_N, 2 * SLC_N + 512)
        base = 2 * SLC_N + 1024
        KRAW = base
        VRAW = KRAW + GRP
        SC1K = VRAW + GRP
        SC2K = SC1K + GRP
        assert SC2K + GRP <= SCR_N, SC2K + GRP
        QRAW = base
        QROT = QRAW + PT
        SC1 = QROT + PT
        SC2 = SC1 + PT
        PB = (SC2 + PT, SC2 + PT + KW)
        PTS = PB[1] + KW
        assert PTS + KW <= SCR_N, PTS + KW
        MSK = base
        Z1T = MSK + 512
        G8 = Z1T + 512
        GSB = G8 + 64
        IDXROW = GSB + 1024
        assert IDXROW + 512 <= SCR_N

        sums3 = E(nc.sbuf_tensor("sums3", [128, 3], F32))
        sumt = E(nc.sbuf_tensor("sumt", [128, 1], F32))
        rcp = E(nc.sbuf_tensor("rcp", [128, 1], F32))
        NSLOT = 4
        stats = E(nc.sbuf_tensor("stats", [128, 6 * (D // 512) * NSLOT], F32))
        mv = E(nc.sbuf_tensor("mv", [128, 2 * NSLOT], F32))
        rstd = E(nc.sbuf_tensor("rstd", [128, NSLOT], F32))
        nbias = E(nc.sbuf_tensor("nbias", [128, NSLOT], F32))
        gsb = scr.f32(GSB, 512, p0=0, p1=NG)
        mx8 = E(nc.sbuf_tensor("mx8", [128, 8], F32))
        idx8 = E(nc.sbuf_tensor("idx8", [128, 8], U32))
        idxf = E(nc.sbuf_tensor("idxf", [128, 1], F32))
        idxrow = scr.bf(IDXROW, PT, p0=0, p1=1)
        umsk = B2[0:MAXR, DC * MAXR + 3 * D: DC * MAXR + 3 * D + OWN]

        PS = E(nc.psum_tensor("PS", [128, 4096], F32))
        PSb = PS[:, :].bitcast(BF16)

        def bank(i):
            return PS[:, i * 512:(i + 1) * 512]

        def bankb(i):
            return PSb[:, i * 1024:(i + 1) * 1024]

        P = Prog(nc, es)

        r_const = Res()
        r_constS = Res()
        r_bank = [Res() for _ in range(8)]
        r_slc = [Res(), Res()]
        r_wrs = [Res(), Res()]
        r_xtm = [Res(), Res()]
        r_xg = [Res(), Res()]
        r_krot = Res()
        r_vtm = Res()
        r_kraw = Res()
        r_vraw = Res()
        r_sc12 = Res()
        r_xfm = Res()
        r_qraw = [Res(), Res()]
        r_qrot = [Res(), Res()]
        r_p = [Res(), Res()]
        r_pts = Res()
        r_D = Res()
        r_xres = Res()
        r_z1 = [Res(), Res()]
        r_z1tm = [Res(), Res()]
        r_sums = Res()
        r_sumt = Res()
        r_rcp = Res()
        r_stats = [Res() for _ in range(4)]
        r_mv = [Res() for _ in range(4)]
        r_rstd = [Res() for _ in range(4)]
        r_nbias = [Res() for _ in range(4)]
        r_gate = Res()
        r_idxrow = Res()
        r_umsk = Res()
        r_msk = Res()
        r_z2out = Res()
        r_y = [Res(), Res()]

        def ld_cast(dst, src):
            return lambda e: e.dma_start(out=dst, in_=src, max_dma_last_dim=8192)

        def ld_plain(dst, src):
            return lambda e: e.dma_start(out=dst, in_=src)

        for dst, src in [(c_id[:, :], idnt), (c_p128[:, :], p128), (c_m1[:, :], m1t),
                         (c_m2[:, :], m2t), (c_ones[:, :], ones_r),
                         (c_g[:, :], g_l), (c_cos[:, :], cos_e), (c_sin[:, :], sin_e)]:
            P.do("gdma", ld_cast(dst, src), writes=[r_const])
        for dst, src in [(c_idf[:, :], idnt), (c_thr[:, :], thr),
                         (c_corr[:, :], corr_l), (c_gbc[:, :], gb_c)]:
            P.do("sdma", ld_plain(dst, src), writes=[r_constS])
        RC = [r_const, r_constS]

        def xg_ap(g, kc, col0, ncol):
            buf = B2 if g % 2 == 0 else B1
            return buf[:, kc * GRP + col0: kc * GRP + col0 + ncol]

        def transpose_batches():
            out = []
            s = 0
            while s < DC:
                e_ = min(s + 8, DC)
                out.append(list(range(s, e_)))
                s = e_
            return out

        ntile_g = GRP // 128
        kv_bk = [0]

        def tile_stage(g, tt):
            ext_t = g * GRP + tt * 128
            if ext_t < WIN:
                tsrc = xh[ext_t:ext_t + 128, :]
            else:
                tsrc = xo[ext_t - WIN:ext_t - WIN + 128, :]
            sl = tt % 2
            xtm = a5.bf(sl * D, D)
            P.do("gdma", ld_cast(xtm, tsrc), writes=[r_xtm[sl]])
            xgbuf = B2 if g % 2 == 0 else B1
            for chunks in transpose_batches():
                bk = 5 + kv_bk[0] % 3
                kv_bk[0] += 1
                fns = []
                for j, dc in enumerate(chunks):
                    fns.append(lambda e, dc=dc, bk=bk, j=j, xtm=xtm: e.transpose(
                        bankb(bk)[:, j * 128:(j + 1) * 128],
                        xtm[:, dc * 128:(dc + 1) * 128], c_id[:, :]))
                P.do("pe", fns, reads=[r_xtm[sl]] + RC, writes=[r_bank[bk]])
                dc0 = chunks[0]
                dap = xgbuf[:, :].rearrange("p (c t) -> p c t", t=GRP)[
                    :, dc0:dc0 + len(chunks), tt * 128:(tt + 1) * 128]
                P.do("dve", lambda e, bk=bk, dap=dap, n=len(chunks): e.tensor_copy(
                    dap, bankb(bk)[:, 0:n * 128].rearrange("p (c t) -> p c t", t=128)),
                    reads=[r_bank[bk]], writes=[r_xg[g % 2]])

        for tt in range(ntile_g):
            tile_stage(0, tt)
        for g in range(HGRP):
            rxg = r_xg[g % 2]
            next_tiles = list(range(ntile_g)) if g + 1 < HGRP else []
            kv_tail = []
            for oc in range(2 * HKV + 1):
                if oc < 2 * HKV:
                    is_k = oc < HKV
                    hk = oc if is_k else oc - HKV
                    sl = oc % 2
                    pjb = oc % 2
                    wsl = scr.bf(SLC[sl], SLC_N)
                    P.do("gdma", ld_cast(wsl, (wk_l if is_k else wv_l)[hk]),
                         writes=[r_slc[sl]])
                    fns = []
                    for kc in range(DC):
                        fns.append(lambda e, kc=kc, wsl=wsl, pjb=pjb, g=g: e.matmul(
                            PS[:, pjb * 512: pjb * 512 + GRP], wsl[:, kc * 128:(kc + 1) * 128],
                            xg_ap(g, kc, 0, GRP),
                            start=(kc == 0), stop=(kc == DC - 1)))
                    P.do("pe", fns, reads=[r_slc[sl], rxg] + RC, writes=[r_bank[pjb]])
                step_t = max(1, (2 * HKV - 1) // ntile_g)
                if next_tiles and oc >= 2 and (oc - 2) % step_t == 0:
                    tile_stage(g + 1, next_tiles.pop(0))
                if next_tiles and oc == 2 * HKV:
                    while next_tiles:
                        tile_stage(g + 1, next_tiles.pop(0))
                if kv_tail:
                    kv_tail.pop(0)()
                if oc >= 2 * HKV:
                    continue

                def kv_stage(is_k=is_k, hk=hk, pjb=pjb, g=g):
                    if is_k:
                        swb = 2 + hk % 2
                        kraw = scr.bf(KRAW, GRP)
                        P.do("act", lambda e, kraw=kraw, pjb=pjb: e.activation(
                            kraw, PS[:, pjb * 512: pjb * 512 + GRP], AF.Copy),
                            reads=[r_bank[pjb]], writes=[r_kraw])
                        P.do("pe", lambda e, kraw=kraw, swb=swb: e.matmul(
                            PS[:, swb * 512: swb * 512 + GRP], c_p128[:, :], kraw,
                            start=True, stop=True),
                            reads=[r_kraw] + RC, writes=[r_bank[swb]])
                        s1 = scr.bf(SC1K, GRP)
                        s2 = scr.bf(SC2K, GRP)
                        t_a = P.do("dve", lambda e, s1=s1, g=g, swb=swb: e.tensor_mul(
                            s1, PS[:, swb * 512: swb * 512 + GRP], c_sin[:, g * GRP:(g + 1) * GRP]),
                            reads=[r_bank[swb]] + RC, writes=[r_sc12])
                        t_b = P.do("dve", lambda e, s2=s2, kraw=kraw, g=g: e.tensor_mul(
                            s2, kraw, c_cos[:, g * GRP:(g + 1) * GRP]),
                            reads=[r_kraw] + RC, writes=[r_sc12])
                        P.do("dve", lambda e, s1=s1, s2=s2, hk=hk, g=g: e.tensor_add(
                            B3[:, hk * EXT + g * GRP: hk * EXT + (g + 1) * GRP], s1, s2),
                            reads=[r_sc12], writes=[r_krot], extra_waits=[t_a, t_b])
                    else:
                        vraw = scr.bf(VRAW, GRP)
                        P.do("act", lambda e, vraw=vraw, pjb=pjb: e.activation(
                            vraw, PS[:, pjb * 512: pjb * 512 + GRP], AF.Copy),
                            reads=[r_bank[pjb]], writes=[r_vraw])
                        fns = []
                        for tt in range(ntile_g):
                            fns.append(lambda e, tt=tt, vraw=vraw: e.transpose(
                                bankb(4)[:, tt * 128:(tt + 1) * 128],
                                vraw[:, tt * 128:(tt + 1) * 128], c_id[:, :]))
                        P.do("pe", fns, reads=[r_vraw] + RC, writes=[r_bank[4]])
                        vdst = B4[:, :].rearrange("p (kt h d) -> p kt h d", h=HKV, d=128)[
                            :, g * ntile_g:(g + 1) * ntile_g, hk, :]
                        P.do("dve", lambda e, vdst=vdst: e.tensor_copy(
                            vdst, bankb(4)[:, 0:ntile_g * 128].rearrange("p (t d) -> p t d", d=128)),
                            reads=[r_bank[4]], writes=[r_vtm])
                kv_tail.append(kv_stage)

        NST6 = 6 * (D // 512)

        def ln_chain(slot, zpre, rzpre, out, rout, extra_reads=(), skip_stats=False):
            st = stats[:, slot * NST6:(slot + 1) * NST6]
            mvs = mv[:, slot * 2:(slot + 1) * 2]
            rs = rstd[:, slot:slot + 1]
            nb = nbias[:, slot:slot + 1]
            nst = D // 512
            if not skip_stats:
                fns = []
                for q5 in range(nst):
                    fns.append(lambda e, q5=q5, zpre=zpre, st=st: e.bn_stats(
                        st[:, q5 * 6:(q5 + 1) * 6], zpre[:, q5 * 512:(q5 + 1) * 512]))
                P.do("dve", fns, reads=[rzpre], writes=[r_stats[slot]])
            P.do("dve", lambda e, st=st, mvs=mvs: e.bn_aggr(mvs, st),
                 reads=[r_stats[slot]], writes=[r_mv[slot]])
            P.do("dve", lambda e, mvs=mvs, rs=rs: e.tensor_scalar_add(
                rs, mvs[:, 1:2], float(c["EPS"])),
                reads=[r_mv[slot]], writes=[r_rstd[slot]])
            P.do("dve", lambda e, rs=rs: e.reciprocal(rs, rs),
                 reads=[r_rstd[slot]], writes=[r_rstd[slot]])
            P.do("act", lambda e, rs=rs: e.activation(rs, rs, AF.Sqrt),
                 reads=[r_rstd[slot]], writes=[r_rstd[slot]])
            P.do("dve", lambda e, mvs=mvs, rs=rs, nb=nb: e.tensor_mul(nb, mvs[:, 0:1], rs),
                 reads=[r_mv[slot], r_rstd[slot]], writes=[r_nbias[slot]])
            P.do("dve", lambda e, nb=nb: e.tensor_scalar_mul(nb, nb, -1.0),
                 reads=[r_nbias[slot]], writes=[r_nbias[slot]])
            P.do("act", lambda e, out=out, zpre=zpre, rs=rs, nb=nb: e.activation(
                out, zpre, AF.Identity, bias=nb, scale=rs),
                reads=[rzpre, r_rstd[slot], r_nbias[slot]] + list(extra_reads),
                writes=[rout])

        sm = c["SM"]
        vview = B4[:, :].rearrange("p (kt h d) -> p kt h d", h=HKV, d=128)
        for r in (r_qraw[0], r_qraw[1], r_qrot[0], r_qrot[1], r_p[0], r_p[1],
                  r_pts, r_wrs[0], r_wrs[1]):
            r.take(r_kraw)
            r.take(r_vraw)
            r.take(r_sc12)
        NOREBUILD = (PT == GRP and WIN % GRP == 0 and not cfg.get("FORCE_REBUILD"))
        for p_i in range(NPASS):
            tok0 = p_i * PT
            g_pass = (WIN + tok0) // GRP
            if NOREBUILD:
                xgb = B2 if g_pass % 2 == 0 else B1
                rxfm_r = r_xg[g_pass % 2]

                def xfm_ap(kc, xgb=xgb):
                    return xgb[:, kc * GRP: kc * GRP + PT]
                XR = Arena(xgb)
                if p_i == 0:
                    DBt = B5
                    r_D.take(r_xtm[0])
                    r_D.take(r_xtm[1])
                else:
                    g_prev = (WIN + (p_i - 1) * PT) // GRP
                    DBt = B2 if g_prev % 2 == 0 else B1
                    r_D.take(r_xres)
                    r_D.take(r_xg[g_prev % 2])
            else:
                rxfm_r = r_xfm

                def xfm_ap(kc):
                    return B2[:, kc * PT: (kc + 1) * PT]
                XR = a2
                DBt = B1
                r_xtm[0].take(r_D)
                r_xtm[1].take(r_D)
                for rr in (r_xtm[0], r_xtm[1]):
                    rr.take(r_xg[0])
                    rr.take(r_xg[1])
                    rr.take(r_z1tm[0])
                    rr.take(r_z1tm[1])
                r_xfm.take(r_xres)
                r_xfm.take(r_xg[0])
                r_xfm.take(r_xg[1])
                xa = a5 if p_i == 0 else a1
                for ts in range(QS):
                    sl = ts % 2
                    xtm = xa.bf(sl * D, D)
                    P.do("gdma", ld_cast(xtm, xo[tok0 + ts * 128: tok0 + (ts + 1) * 128, :]),
                         writes=[r_xtm[sl]])
                    for bi, chunks in enumerate(transpose_batches()):
                        bk = (ts % 2) * 4 + bi % 4
                        fns = []
                        for j, dc in enumerate(chunks):
                            fns.append(lambda e, dc=dc, bk=bk, j=j, xtm=xtm: e.transpose(
                                bankb(bk)[:, j * 128:(j + 1) * 128],
                                xtm[:, dc * 128:(dc + 1) * 128], c_id[:, :]))
                        P.do("pe", fns, reads=[r_xtm[sl]] + RC, writes=[r_bank[bk]])
                        dc0 = chunks[0]
                        dap = B2[:, :].rearrange("p (c t) -> p c t", t=PT)[
                            :, dc0:dc0 + len(chunks), ts * 128:(ts + 1) * 128]
                        P.do("dve", lambda e, bk=bk, dap=dap, n=len(chunks): e.tensor_copy(
                            dap, bankb(bk)[:, 0:n * 128].rearrange("p (c t) -> p c t", t=128)),
                            reads=[r_bank[bk]], writes=[r_xfm])
                r_D.take(r_xtm[0])
                r_D.take(r_xtm[1])
            tailsT = []
            tailsA = []
            qsc = [0]

            QR = (QRAW, 2 * SLC_N)
            QT = (QROT, 2 * SLC_N + 512)

            def emit_A_load(h):
                sl = h % 2
                wsl = scr.bf(SLC[sl], SLC_N)
                P.do("gdma", ld_cast(wsl, wq_l[h]), writes=[r_slc[sl]])

            def emit_A_piece(h, j, npiece):
                sl = h % 2
                wsl = scr.bf(SLC[sl], SLC_N)
                per = DC // npiece
                fns = []
                for kc in range(j * per, (j + 1) * per):
                    fns.append(lambda e, kc=kc, wsl=wsl, xfm_ap=xfm_ap: e.matmul(
                        PS[:, 7 * 512: 7 * 512 + PT], wsl[:, kc * 128:(kc + 1) * 128],
                        xfm_ap(kc),
                        start=(kc == 0), stop=(kc == DC - 1)))
                P.do("pe", fns, reads=[r_slc[sl], rxfm_r],
                     writes=[r_bank[7]] if j in (0, npiece - 1) else [])

            def emit_B1(h):
                hp = h % 2
                qraw = scr.bf(QR[hp], PT)
                P.do("act", lambda e, qraw=qraw: e.activation(
                    qraw, PS[:, 7 * 512: 7 * 512 + PT], AF.Copy, scale=float(sm)),
                    reads=[r_bank[7]], writes=[r_qraw[hp]])

            def emit_B2(h):
                hp = h % 2
                qraw = scr.bf(QR[hp], PT)
                qrot = scr.bf(QT[hp], PT)
                P.do("pe", lambda e, qraw=qraw: e.matmul(
                    PS[:, 7 * 512: 7 * 512 + PT], c_p128[:, :], qraw,
                    start=True, stop=True),
                    reads=[r_qraw[hp]] + RC, writes=[r_bank[7]])
                s1 = scr.bf(SC1, PT)
                s2 = scr.bf(SC2, PT)
                cs0 = WIN + tok0
                t_a = P.do("dve", lambda e, s1=s1, cs0=cs0: e.tensor_mul(
                    s1, PS[:, 7 * 512: 7 * 512 + PT], c_sin[:, cs0:cs0 + PT]),
                    reads=[r_bank[7]] + RC, writes=[r_sc12])
                t_b = P.do("dve", lambda e, s2=s2, qraw=qraw, cs0=cs0: e.tensor_mul(
                    s2, qraw, c_cos[:, cs0:cs0 + PT]),
                    reads=[r_qraw[hp]] + RC, writes=[r_sc12])
                P.do("dve", lambda e, qrot=qrot, s1=s1, s2=s2: e.tensor_add(qrot, s1, s2),
                     reads=[r_sc12], writes=[r_qrot[hp]], extra_waits=[t_a, t_b])

            def emit_C(h, s4):
                hk = h // G
                hp = h % 2
                qrot = scr.bf(QT[hp], PT)
                sg = p_i * QS + s4
                elo = sg * 128
                par = qsc[0] % 2
                qsc[0] += 1
                bs = 0 if par == 0 else 3
                pb = scr.bf(PB[par], KW)
                rp = r_p[par]
                schunks = []
                off = 0
                while off < KW:
                    wdt = min(512, KW - off)
                    schunks.append((off, wdt))
                    off += wdt
                nch = len(schunks)
                av_bk = (3 - bs) + nch - 1
                t9_bk = bs + nch - 1
                av_off = schunks[-1][1]
                n_extra = max(0, KW // 128 - 8)
                assert 2 * (av_off + 128) + n_extra * 128 <= 1024
                sbanks = [r_bank[bs + ci] for ci in range(nch)]
                for ci, (off, wdt) in enumerate(schunks):
                    bkk = bs + ci
                    has_m1 = off == 0
                    has_m2 = off + wdt == KW
                    fns = [lambda e, off=off, wdt=wdt, bkk=bkk, s4=s4, hk=hk, elo=elo, qrot=qrot, st=(not has_m1 and not has_m2):
                           e.matmul(PS[:, bkk * 512: bkk * 512 + wdt],
                                    qrot[:, s4 * 128:(s4 + 1) * 128],
                                    B3[:, hk * EXT + elo + off: hk * EXT + elo + off + wdt],
                                    start=True, stop=st)]
                    if has_m1:
                        fns.append(lambda e, bkk=bkk, st=(not has_m2): e.matmul(
                            PS[:, bkk * 512: bkk * 512 + 128], c_id[:, :], c_m1[:, :],
                            start=False, stop=st))
                    if has_m2:
                        fns.append(lambda e, bkk=bkk, off=off, wdt=wdt: e.matmul(
                            PS[:, bkk * 512 + wdt - 128: bkk * 512 + wdt],
                            c_id[:, :], c_m2[:, :], start=False, stop=True))
                    P.do("pe", fns, reads=[r_qrot[hp], r_krot] + RC, writes=[r_bank[bkk]])

                def emit_post(bs=bs, pb=pb, rp=rp, sg=sg, sbanks=sbanks):
                    P.do("act", lambda e, bs=bs, pb=pb: e.activation(
                        pb[:, 0:KW], PS[:, bs * 512: bs * 512 + KW],
                        AF.Exp, accum_out=sums3[:, 0:1]),
                        reads=sbanks, writes=[rp, r_sums])
                    P.do("dve", lambda e, sg=sg: e.tensor_add(
                        sumt[:, :], sums3[:, 0:1], c_corr[:, sg:sg + 1]),
                        reads=[r_sums] + RC, writes=[r_sumt])
                    P.do("dve", lambda e: e.reciprocal(rcp[:, :], sumt[:, :]),
                         reads=[r_sumt], writes=[r_rcp])
                    P.do("dve", lambda e, pb=pb: e.tensor_scalar_mul(pb, pb, rcp[:, :]),
                         reads=[rp, r_rcp], writes=[rp])

                def tailT(h=h, s4=s4, sg=sg, hk=hk, pb=pb, rp=rp,
                          av_bk=t9_bk, av_off=av_off):
                    nfull = KW // 128
                    pts = scr.bf(PTS, KW)
                    n6 = min(8, nfull)
                    fns = []
                    for j in range(n6):
                        fns.append(lambda e, j=j, pb=pb: e.transpose(
                            bankb(6)[:, j * 128:(j + 1) * 128],
                            pb[:, j * 128:(j + 1) * 128], c_id[:, :]))
                    extra = []
                    for j in range(n6, nfull):
                        xoff = 2 * (av_off + 128) + (j - n6) * 128
                        extra.append(xoff)
                        fns.append(lambda e, j=j, pb=pb, xoff=xoff, av_bk=av_bk: e.transpose(
                            bankb(av_bk)[:, xoff:xoff + 128],
                            pb[:, j * 128:(j + 1) * 128], c_id[:, :]))
                    P.do("pe", fns, reads=[rp] + RC, writes=[r_bank[6], r_bank[av_bk]])
                    P.do("dve", lambda e, pts=pts, n6=n6: e.tensor_copy(
                        pts[:, 0:n6 * 128], bankb(6)[:, 0:n6 * 128]),
                        reads=[r_bank[6]], writes=[r_pts])
                    if nfull > n6:
                        P.do("dve", lambda e, pts=pts, n6=n6, nfull=nfull, av_bk=av_bk,
                             x0=extra[0]: e.tensor_copy(
                            pts[:, n6 * 128:nfull * 128],
                            bankb(av_bk)[:, x0:x0 + (nfull - n6) * 128]),
                            reads=[r_bank[av_bk]], writes=[r_pts])

                def tailA(h=h, s4=s4, sg=sg, hk=hk,
                          av_bk=av_bk, av_off=av_off):
                    nfull = KW // 128
                    pts = scr.bf(PTS, KW)
                    fns = []
                    for j in range(nfull):
                        kt = sg + j
                        fns.append(lambda e, j=j, kt=kt, hk=hk, pts=pts:
                                   e.matmul(PS[:, av_bk * 512 + av_off: av_bk * 512 + av_off + 128],
                                            vview[:, kt, hk, :],
                                            pts[:, j * 128:(j + 1) * 128],
                                            start=(j == 0), stop=(j == nfull - 1)))
                    P.do("pe", fns, reads=[r_pts, r_vtm], writes=[r_bank[av_bk]])
                    P.do("act", lambda e, DBt=DBt: e.activation(
                        DBt[:, h * PT + s4 * 128: h * PT + (s4 + 1) * 128],
                        PS[:, av_bk * 512 + av_off: av_bk * 512 + av_off + 128],
                        AF.Copy),
                        reads=[r_bank[av_bk]], writes=[r_D])
                return emit_post, tailT, tailA

            emit_A_load(0)
            for j in range(QS):
                emit_A_piece(0, j, QS)
            emit_B1(0)
            emit_B2(0)
            for h in range(HQ):
                if h + 1 < HQ:
                    emit_A_load(h + 1)
                for s4 in range(QS):
                    post, tT, tA = emit_C(h, s4)
                    post()
                    if len(tailsA) >= 2:
                        tailsA.pop(0)()
                    if h + 1 < HQ:
                        emit_A_piece(h + 1, s4, QS)
                    if s4 == QS - 1 and h + 1 < HQ:
                        emit_B1(h + 1)
                    if tailsT:
                        tailsT.pop(0)()
                    tailsT.append(tT)
                    tailsA.append(tA)
                    if s4 == QS - 1 and h + 1 < HQ:
                        emit_B2(h + 1)
            while tailsT or tailsA:
                if tailsA:
                    tailsA.pop(0)()
                if tailsT:
                    tailsT.pop(0)()
            if cfg.get("DEBUG_D") and p_i == 0:
                P.do("gdma", lambda e, DBt=DBt: e.dma_start(
                    out=y[0:128, :].rearrange("p (a b) -> p a b", a=2) if False else y[0:256, :].rearrange("(a p) m -> p a m", p=128),
                    in_=DBt[:, 0:HQ * PT].rearrange("p (a m) -> p a m", a=2),
                    max_dma_last_dim=8192),
                    reads=[r_D], writes=[r_y[0]])
            r_xres.take(rxfm_r)
            for ts_ in range(QS):
                P.do("gdma", ld_cast(XR.bf(ts_ * D, D),
                                     xo[tok0 + ts_ * 128: tok0 + (ts_ + 1) * 128, :]),
                     writes=[r_xres])
            ch_sz = max(1, min(8, HQ, SLC_N // 512))
            n_ch = HQ // ch_sz
            for ob in range(OB):
                lastw = None
                for ch in range(n_ch):
                    wch = scr.bf(SLC[ch % 2], ch_sz * 512)
                    P.do("gdma", lambda e, wch=wch, ob=ob, ch=ch: e.dma_start(
                        out=wch[:, :].rearrange("p (s m) -> p s m", m=512),
                        in_=wo_r[ob, ch * ch_sz:(ch + 1) * ch_sz].rearrange("s p m -> p s m"),
                        max_dma_last_dim=8192),
                        writes=[r_slc[ch % 2]])
                    for kcl in range(ch_sz):
                        kc = ch * ch_sz + kcl
                        fns = []
                        for ts in range(QS):
                            fns.append(lambda e, ts=ts, kc=kc, wch=wch, kcl=kcl, DBt=DBt: e.matmul(
                                bank(ts), DBt[:, kc * PT + ts * 128: kc * PT + (ts + 1) * 128],
                                wch[:, kcl * 512:(kcl + 1) * 512],
                                start=(kc == 0), stop=(kc == HQ - 1)))
                        wset = [r_bank[t] for t in range(QS)] if kc in (0, HQ - 1) else []
                        lastw = P.do("pe", fns, reads=[r_slc[ch % 2], r_D], writes=wset)
                for ts in range(QS):
                    xres_sl = XR.bf(ts * D + ob * 512, 512)
                    P.do("dve", lambda e, ts=ts, xres_sl=xres_sl: e.tensor_add(
                        xres_sl, bank(ts), xres_sl),
                        reads=[r_bank[ts], r_xres], writes=[r_xres],
                        extra_waits=[lastw])
                    slot = ts % NSLOT
                    P.do("dve", lambda e, xres_sl=xres_sl, slot=slot, ob=ob: e.bn_stats(
                        stats[:, slot * NST6 + ob * 6: slot * NST6 + (ob + 1) * 6], xres_sl),
                        reads=[r_xres], writes=[r_stats[slot]])
            z1buf = B5 if p_i == 0 else B3
            rz1 = r_z1[p_i]
            if p_i == 0:
                rz1.take(r_xtm[0])
                rz1.take(r_xtm[1])
            else:
                rz1.take(r_krot)
            if NOREBUILD and p_i == 0:
                rz1.take(r_D)
            DBGPRE_ONLY = cfg.get("DEBUG_PRE") and not cfg.get("DEBUG_Z1")
            if not NOREBUILD:
                r_z1tm[0].take(r_D)
                r_z1tm[1].take(r_D)
            for ts in range(QS):
                zpre = XR.bf(ts * D, D)
                if NOREBUILD:
                    z1tm = zpre
                    rzt = r_xres
                else:
                    z1tm = a1.bf((ts % 2) * D, D)
                    rzt = r_z1tm[ts % 2]
                if cfg.get("DEBUG_PRE"):
                    P.do("gdma", lambda e, zpre=zpre, tok0=tok0, ts=ts: e.dma_start(
                        out=y[tok0 + ts * 128: tok0 + (ts + 1) * 128, :],
                        in_=zpre, max_dma_last_dim=8192),
                        reads=[r_xres], writes=[r_y[0]])
                ln_chain(ts % NSLOT, zpre, r_xres, z1tm, rzt, skip_stats=True)
                if cfg.get("DEBUG_Z1"):
                    P.do("gdma", lambda e, z1tm=z1tm, tok0=tok0, ts=ts: e.dma_start(
                        out=y[tok0 + ts * 128: tok0 + (ts + 1) * 128, :],
                        in_=z1tm, max_dma_last_dim=8192),
                        reads=[rzt], writes=[r_y[0]])
                for bi, chunks in enumerate(transpose_batches()):
                    bk = 4 + ((ts % 2) * 2 + bi) % 4
                    fns = []
                    for j, dc in enumerate(chunks):
                        fns.append(lambda e, dc=dc, bk=bk, j=j, z1tm=z1tm: e.transpose(
                            bankb(bk)[:, j * 128:(j + 1) * 128],
                            z1tm[:, dc * 128:(dc + 1) * 128], c_id[:, :]))
                    P.do("pe", fns, reads=[rzt] + RC, writes=[r_bank[bk]])
                    dc0 = chunks[0]
                    dap = z1buf[:, :].rearrange("p (c t) -> p c t", t=PT)[
                        :, dc0:dc0 + len(chunks), ts * 128:(ts + 1) * 128]
                    P.do("dve", lambda e, bk=bk, dap=dap, n=len(chunks): e.tensor_copy(
                        dap, bankb(bk)[:, 0:n * 128].rearrange("p (c t) -> p c t", t=128)),
                        reads=[r_bank[bk]], writes=[rz1])

        if cfg.get("DEBUG_Z1") or cfg.get("DEBUG_PRE"):
            for ry in r_y:
                if ry.dmasem is not None:
                    ysem = P.semh[ry.dmasem]
                    P.raw("sdma", lambda e, ysem=ysem, v=P.cnt[ry.dmasem]: e.wait_ge(ysem, v))
            return _finish(nc, P, es)
        cA = a2.bf(0, DC * MAXR)
        cB_off = DC * MAXR
        cB = a2.bf(cB_off, D, p0=0, p1=MAXR)
        r_cA = Res()
        r_cA.take(r_xres)
        r_cA.take(r_D)
        r_cB = Res()
        r_cB.take(r_xres)
        r_cB.take(r_D)
        P.do("gdma", ld_cast(cA, A_l), writes=[r_cA])
        P.do("gdma", ld_cast(cB, B_l), writes=[r_cB])

        def z1_ap(dc, col0, ncol):
            buf = B5 if col0 < PT else B3
            base = col0 if col0 < PT else col0 - PT
            return buf[:, dc * PT + base: dc * PT + base + ncol]

        c_brow = B2[0:1, cB_off + D: cB_off + 2 * D]
        r_brow = Res()
        r_brow.take(r_xres)
        r_brow.take(r_D)
        for rr in (r_msk, r_sc12):
            for src_r in (r_slc[0], r_slc[1], r_qraw[0], r_qraw[1], r_qrot[0],
                          r_qrot[1], r_p[0], r_p[1], r_pts, r_kraw, r_vraw):
                rr.take(src_r)
        r_msk.take(r_sc12)
        r_umsk.take(r_D)
        r_umsk.take(r_xres)
        P.do("gdma", ld_cast(c_brow, b_row), writes=[r_brow])
        for th in range(NPASS):
            rz = r_z1[th]
            fns = []
            for dc in range(DC):
                fns.append(lambda e, dc=dc, th=th: e.matmul(
                    PS[0:MAXR, 0:PT], cA[:, dc * MAXR:(dc + 1) * MAXR],
                    z1_ap(dc, th * PT, PT), start=(dc == 0), stop=(dc == DC - 1)))
            P.do("pe", fns, reads=[r_cA, rz], writes=[r_bank[0]])
            fns = []
            for dc in range(DC):
                fns.append(lambda e, dc=dc, th=th: e.matmul(
                    PS[0:NG, 512:512 + PT], c_g[:, dc * NG:(dc + 1) * NG],
                    z1_ap(dc, th * PT, PT), start=(dc == 0), stop=(dc == DC - 1)))
            P.do("pe", fns, reads=[rz] + RC, writes=[r_bank[1]])
            P.do("act", lambda e: e.activation(gsb[:, 0:PT], PS[0:NG, 512:512 + PT],
                                               AF.Identity, bias=c_gbc[:, :]),
                 reads=[r_bank[1]] + RC, writes=[r_gate])
            for ts in range(QS):
                P.do("pe", lambda e, ts=ts: e.transpose(
                    PS[0:128, 1024 + ts * 4: 1024 + (ts + 1) * 4],
                    gsb[:, ts * 128:(ts + 1) * 128], c_idf[0:NG, 0:NG]),
                    reads=[r_gate] + RC, writes=[r_bank[2]])
                g8 = scr.f32(G8, 8)
                P.do("dve", lambda e, g8=g8: e.memset(g8, -1e30), writes=[r_msk])
                P.do("dve", lambda e, ts=ts, g8=g8: e.tensor_copy(
                    g8[:, 0:NG], PS[0:128, 1024 + ts * 4:1024 + (ts + 1) * 4]),
                    reads=[r_bank[2], r_msk], writes=[r_msk])
                P.do("dve", lambda e, g8=g8: e.max(mx8[:, :], g8),
                     reads=[r_msk], writes=[r_gate])
                P.do("dve", lambda e, g8=g8: e.max_index(idx8[:, :], mx8[:, :], g8),
                     reads=[r_gate, r_msk], writes=[r_gate])
                P.do("dve", lambda e: e.tensor_copy(idxf[:, :], idx8[:, 0:1]),
                     reads=[r_gate], writes=[r_gate])
                P.do("pe", lambda e, ts=ts: e.transpose(
                    PS[0:1, 1536 + ts * 128: 1536 + (ts + 1) * 128], idxf[:, :], c_idf[:, :]),
                    reads=[r_gate] + RC, writes=[r_bank[3]])
                P.do("dve", lambda e, ts=ts: e.tensor_copy(
                    idxrow[:, ts * 128:(ts + 1) * 128],
                    PS[0:1, 1536 + ts * 128:1536 + (ts + 1) * 128]),
                    reads=[r_bank[3]], writes=[r_idxrow])
            P.do("pe", lambda e: e.matmul(PS[0:MAXR, 1024:1024 + PT],
                                          c_ones[:, 0:MAXR], idxrow[:, 0:PT],
                                          start=True, stop=True),
                 reads=[r_idxrow] + RC, writes=[r_bank[2]])
            msk = scr.bf(MSK, 512, p0=0, p1=MAXR)
            P.do("dve", lambda e, msk=msk: e.tensor_scalar(
                msk[:, 0:PT], PS[0:MAXR, 1024:1024 + PT], c_thr[:, :], None, ALU.is_ge),
                reads=[r_bank[2]] + RC, writes=[r_msk])
            P.do("dve", lambda e, th=th, msk=msk: e.tensor_mul(
                umsk[:, th * PT:(th + 1) * PT], PS[0:MAXR, 0:PT], msk[:, 0:PT]),
                reads=[r_bank[0], r_msk], writes=[r_umsk])

        last_g = (WIN + (NPASS - 1) * PT) // GRP
        INPLACE_T8 = (NSUB - QS) if (NOREBUILD and last_g % 2 == 1) else NSUB

        def z2pre_ap(t8, col0, ncol):
            if NOREBUILD:
                if t8 < INPLACE_T8:
                    return a4.bf(t8 * D + col0, ncol)
                return a1.bf((t8 - INPLACE_T8) * D + col0, ncol)
            if t8 < 4:
                return a1.bf(t8 * D + col0, ncol)
            if t8 < 7:
                return a4.bf((t8 - 3) * D + col0, ncol)
            return a2.bf(cB_off + 2 * D + col0, ncol)

        r_z2 = [Res() for _ in range(NSUB)]
        for t8 in range(NSUB):
            if NOREBUILD:
                if t8 < INPLACE_T8:
                    r_z2[t8].take(r_vtm)
                else:
                    r_z2[t8].take(r_xres)
            elif t8 < 4:
                r_z2[t8].take(r_D)
                r_z2[t8].take(r_xres)
                r_z2[t8].take(r_z1tm[0])
                r_z2[t8].take(r_z1tm[1])
            elif t8 < 7:
                r_z2[t8].take(r_vtm)
            else:
                r_z2[t8].take(r_cB)
        r_z2out2 = [r_z2out, Res()]
        r_z2out2[1].take(r_z2out)

        def emit_ln2(t8):
            zpre = z2pre_ap(t8, 0, D)
            rzo = r_z2out2[t8 % 2]
            if t8 < PT // 128:
                rzo.take(r_z1[0])
                zout = a5.f32((t8 % 2) * 2 * D, D)
            else:
                rzo.take(r_z1[1])
                zout = a3.f32((t8 % 2) * 2 * D, D)
            ln_chain(t8 % NSLOT, zpre, r_z2[t8], zout, rzo, skip_stats=True)
            P.do("sdma", lambda e, t8=t8, zout=zout: e.dma_start(
                out=y[t8 * 128:(t8 + 1) * 128, :], in_=zout),
                reads=[rzo], writes=[r_y[t8 % 2]])

        waves = [list(range(s, min(s + 4, NSUB))) for s in range(0, NSUB, 4)]
        ch_f = max(1, min(8, DC, SLC_N // 512))
        n_chf = DC // ch_f
        for wave in waves:
            for ob in range(OB):
                for ch in range(n_chf):
                    wch = scr.bf(SLC[ch % 2], ch_f * 512)
                    P.do("gdma", lambda e, wch=wch, ob=ob, ch=ch: e.dma_start(
                        out=wch[:, :].rearrange("p (s m) -> p s m", m=512),
                        in_=W_r[ob, ch * ch_f:(ch + 1) * ch_f].rearrange("s p m -> p s m"),
                        max_dma_last_dim=8192),
                        writes=[r_slc[ch % 2]])
                    for dcl in range(ch_f):
                        dc = ch * ch_f + dcl
                        fns = []
                        for tj, t8 in enumerate(wave):
                            fns.append(lambda e, tj=tj, t8=t8, dc=dc, wch=wch, dcl=dcl: e.matmul(
                                bank(tj), z1_ap(dc, t8 * 128, 128),
                                wch[:, dcl * 512:(dcl + 1) * 512],
                                start=(dc == 0), stop=False))
                        P.do("pe", fns, reads=[r_slc[ch % 2], r_z1[0], r_z1[1]],
                             writes=[r_bank[tj] for tj in range(len(wave))] if dc == 0 else [])
                fns = []
                for tj, t8 in enumerate(wave):
                    fns.append(lambda e, tj=tj, t8=t8, ob=ob: e.matmul(
                        bank(tj), umsk[:, t8 * 128:(t8 + 1) * 128],
                        cB[:, ob * 512:(ob + 1) * 512], start=False, stop=False))
                    fns.append(lambda e, tj=tj, ob=ob: e.matmul(
                        bank(tj), c_ones[:, 0:128], c_brow[:, ob * 512:(ob + 1) * 512],
                        start=False, stop=True))
                tk_g = P.do("pe", fns, reads=[r_umsk, r_cB, r_brow] + RC,
                            writes=[r_bank[tj] for tj in range(len(wave))])
                for tj, t8 in enumerate(wave):
                    noc = min(4, DC - ob * 4)
                    if NOREBUILD and t8 >= INPLACE_T8:
                        P.do("dve", lambda e, tj=tj, t8=t8, ob=ob, noc=noc: e.tensor_add(
                            z2pre_ap(t8, ob * 512, noc * 128),
                            PS[:, tj * 512: tj * 512 + noc * 128],
                            z2pre_ap(t8, ob * 512, noc * 128)),
                            reads=[r_bank[tj], r_xres], writes=[r_z2[t8]],
                            extra_waits=[tk_g])
                    else:
                        fns = []
                        for j in range(noc):
                            oc = ob * 4 + j
                            fns.append(lambda e, j=j, oc=oc, t8=t8: e.transpose(
                                bankb(4)[:, j * 128:(j + 1) * 128],
                                z1_ap(oc, t8 * 128, 128), c_id[:, :]))
                        P.do("pe", fns, reads=[r_z1[0], r_z1[1]] + RC, writes=[r_bank[4]],
                             extra_waits=[tk_g])
                        z1t = scr.bf(Z1T, 512)
                        P.do("dve", lambda e, z1t=z1t, noc=noc: e.tensor_copy(
                            z1t[:, 0:noc * 128], bankb(4)[:, 0:noc * 128]),
                            reads=[r_bank[4]], writes=[r_sc12])
                        P.do("dve", lambda e, tj=tj, t8=t8, ob=ob, z1t=z1t, noc=noc: e.tensor_add(
                            z2pre_ap(t8, ob * 512, noc * 128), PS[:, tj * 512: tj * 512 + noc * 128],
                            z1t[:, 0:noc * 128]),
                            reads=[r_bank[tj], r_sc12], writes=[r_z2[t8]],
                            extra_waits=[tk_g])
                    slot = t8 % NSLOT
                    P.do("dve", lambda e, t8=t8, slot=slot, ob=ob: e.bn_stats(
                        stats[:, slot * NST6 + ob * 6: slot * NST6 + (ob + 1) * 6],
                        z2pre_ap(t8, ob * 512, 512)),
                        reads=[r_z2[t8]], writes=[r_stats[slot]])
            if ob == OB - 1:
                for t8w in wave:
                    emit_ln2(t8w)

        for ry in r_y:
            ysem = P.semh[ry.dmasem]
            final_cnt = P.cnt[ry.dmasem]
            P.raw("sdma", lambda e, ysem=ysem, v=final_cnt: e.wait_ge(ysem, v))
        return _finish(nc, P, es)


def _finish(nc, P, es):
    if True:
        block = es.enter_context(nc.Block())

        @block.tensor
        def _(t):
            for fn in P.items["pe"]:
                fn(t)

        @block.scalar
        def _(sc):
            for fn in P.items["act"]:
                fn(sc)

        @block.vector
        def _(v):
            for fn in P.items["dve"]:
                fn(v)

        @block.gpsimd
        def _(g):
            for fn in P.items["gdma"]:
                fn(g)

        @block.sync
        def _(s):
            for fn in P.items["sdma"]:
                fn(s)
    return nc


def host_constants(cfg):
    c = derived(cfg)
    MAXR = c["MAXR"]
    out = {}
    out["idnt"] = np.eye(128, dtype=np.float32)
    pm = np.zeros((128, 128), dtype=np.float32)
    half = c["HD"] // 2
    for i in range(128):
        pm[i, (i + half) % c["HD"]] = 1.0
    out["p128"] = pm
    BIG = -30720.0
    i_ = np.arange(128)[:, None]
    j_ = np.arange(128)[None, :]
    out["m1t"] = np.where(j_ > i_, 0.0, BIG).astype(np.float32)
    out["m2t"] = np.where(j_ <= i_, 0.0, BIG).astype(np.float32)
    th = np.zeros((MAXR, 1), dtype=np.float32)
    for j in range(MAXR):
        k = 0
        while j >= cfg["RANKS"][k]:
            k += 1
        th[j, 0] = k - 0.5
    out["thr"] = th
    out["ones_r"] = np.ones((1, 512), dtype=np.float32)
    return out


def host_percore(cfg, core, x_full):
    c = derived(cfg)
    D, OWN, WIN, NSUB, HD = c["D"], c["OWN"], c["WIN"], c["NSUB"], c["HD"]
    cb, ci = core // c["NCORE_PER_B"], core % c["NCORE_PER_B"]
    t0 = ci * OWN
    xo = np.ascontiguousarray(x_full[cb, t0:t0 + OWN, :]).astype(np.float32)
    if ci == 0:
        xhalo = np.zeros((WIN, D), dtype=np.float32)
    else:
        xhalo = np.ascontiguousarray(x_full[cb, t0 - WIN:t0, :]).astype(np.float32)
    pos = np.arange(t0 - WIN, t0 + OWN, dtype=np.float64)
    pos = np.clip(pos, 0, None)
    half = HD // 2
    inv = cfg["THETA"] ** (-np.arange(half, dtype=np.float64) / half)
    ang = inv[:, None] * pos[None, :]
    cos = np.cos(ang)
    sin = np.sin(ang)
    cos_e = np.concatenate([cos, cos], axis=0).astype(np.float32)
    sin_e = np.concatenate([-sin, sin], axis=0).astype(np.float32)
    corr = np.zeros((128, NSUB), dtype=np.float32)
    if ci == 0:
        ir = np.arange(128)
        for sg in range(NSUB):
            L = WIN - 128 * sg
            tri = np.clip(np.minimum(128, L) - ir - 1, 0, None)
            mid = max(0, min(WIN, L) - 128)
            corr[:, sg] = -(tri + mid)
    return dict(xo=xo, xh=xhalo, cos_e=cos_e, sin_e=sin_e, corr_l=corr)


def host_weights(cfg, W):
    c = derived(cfg)
    D, DC, HQ, HKV, OB = c["D"], c["DC"], c["HQ"], c["HKV"], c["OB"]
    MAXR, NG = c["MAXR"], c["NG"]
    out = {}

    def lslice(w):
        K, M = w.shape
        r = w.reshape(K // 128, 128, M // 128, 128)
        return np.ascontiguousarray(r.transpose(2, 1, 0, 3)).reshape(
            M // 128, 128, (K // 128) * 128).astype(np.float32)

    out["wq_l"] = lslice(W["wq"])
    out["wk_l"] = lslice(W["wk"])
    out["wv_l"] = lslice(W["wv"])
    wo = W["wo"]
    r = wo.reshape(DC, 128, OB, 512)
    out["wo_r"] = np.ascontiguousarray(r.transpose(2, 0, 1, 3)).astype(np.float32)
    Wf = W["W"]
    r = Wf.reshape(OB, 512, DC, 128)
    out["W_r"] = np.ascontiguousarray(r.transpose(0, 2, 3, 1)).astype(np.float32)
    A = W["A"]
    r = A.reshape(MAXR, DC, 128)
    out["A_l"] = np.ascontiguousarray(r.transpose(2, 1, 0)).reshape(128, DC * MAXR).astype(np.float32)
    out["B_l"] = np.ascontiguousarray(W["B"].T).astype(np.float32)
    g = W["gate_w"].reshape(NG, DC, 128)
    out["g_l"] = np.ascontiguousarray(g.transpose(2, 1, 0)).reshape(128, DC * NG).astype(np.float32)
    out["gb_c"] = np.ascontiguousarray(W["gate_b"].reshape(NG, 1)).astype(np.float32)
    out["b_row"] = np.ascontiguousarray(W["b"].reshape(1, D)).astype(np.float32)
    return out


def make_in_maps(cfg, x, wdict):
    c = derived(cfg)
    consts = host_constants(cfg)
    wmats = host_weights(cfg, wdict)
    n_cores = c["B"] * c["NCORE_PER_B"]
    in_maps = []
    for core in range(n_cores):
        m = dict(consts)
        m.update(wmats)
        m.update(host_percore(cfg, core, x))
        in_maps.append(m)
    return in_maps


def kernel(L0_attn_x, L0_attn_wq, L0_attn_wk, L0_attn_wv, L0_attn_wo,
           L0_ffn_W, L0_ffn_b, L0_ffn_A, L0_ffn_B, L0_ffn_gate_w, L0_ffn_gate_b):
    cfg = REAL
    c = derived(cfg)
    x = np.asarray(L0_attn_x, dtype=np.float32)
    in_maps = make_in_maps(cfg, x, dict(
        wq=np.asarray(L0_attn_wq, np.float32), wk=np.asarray(L0_attn_wk, np.float32),
        wv=np.asarray(L0_attn_wv, np.float32), wo=np.asarray(L0_attn_wo, np.float32),
        W=np.asarray(L0_ffn_W, np.float32), b=np.asarray(L0_ffn_b, np.float32),
        A=np.asarray(L0_ffn_A, np.float32), B=np.asarray(L0_ffn_B, np.float32),
        gate_w=np.asarray(L0_ffn_gate_w, np.float32),
        gate_b=np.asarray(L0_ffn_gate_b, np.float32)))
    nc = build_program(cfg)
    res = run_bass_kernel_spmd(nc, in_maps, core_ids=list(range(8)))
    B, S, Dm = x.shape
    out = np.empty((B, S, Dm), dtype=np.float32)
    for core in range(8):
        cb, ci = core // c["NCORE_PER_B"], core % c["NCORE_PER_B"]
        out[cb, ci * c["OWN"]:(ci + 1) * c["OWN"], :] = res.results[core]["y"]
    return out


# version 29
# speedup vs baseline: 1.0077x; 1.0077x over previous
import contextlib
import numpy as np
import concourse.bass as bass
import concourse.mybir as mybir
from concourse.bass_utils import run_bass_kernel_spmd

F32 = mybir.dt.float32
BF16 = mybir.dt.bfloat16
U32 = mybir.dt.uint32
AF = mybir.ActivationFunctionType
ALU = mybir.AluOpType

REAL = dict(
    B=2, S=4096, D=4096, HQ=32, HKV=8, HD=128, WIN=1024,
    OWN=1024, RANKS=(8, 16, 32, 64), EPS=1e-5, THETA=10000.0,
)


def derived(cfg):
    c = dict(cfg)
    c["G"] = c["HQ"] // c["HKV"]
    c["DC"] = c["D"] // 128
    c["EXT"] = c["OWN"] + c["WIN"]
    c["KT"] = c["EXT"] // 128
    c["NPASS"] = 2
    c["PT"] = c["OWN"] // c["NPASS"]
    c["QS"] = c["PT"] // 128
    c["NSUB"] = c["OWN"] // 128
    c["KW"] = c["WIN"] + 128
    c["OB"] = c["D"] // 512
    c["MAXR"] = max(c["RANKS"])
    c["NG"] = len(c["RANKS"])
    c["SM"] = c["HD"] ** -0.5
    c["NCORE_PER_B"] = c["S"] // c["OWN"]
    c["GRP"] = cfg.get("GRP_OVR") or min(512, c["EXT"])
    c["HGRP"] = c["EXT"] // c["GRP"]
    return c


class Res:
    __slots__ = ("w", "r", "dmasem")

    def __init__(self):
        self.w = None
        self.r = {}
        self.dmasem = None

    def take(self, other):
        toks = list(other.r.items())
        if other.w is not None:
            toks.append(other.w)
        for q, v in toks:
            if self.r.get(q, 0) < v:
                self.r[q] = v


class Prog:
    QUEUES = ("pe", "act", "dve", "gdma", "sdma", "pool")
    DMAQ = ("gdma", "sdma")

    @staticmethod
    def stream(q):
        return "gdma" if q == "pool" else q

    def __init__(self, nc, es):
        self.nc = nc
        self.es = es
        self.semh = {}
        self.cnt = {}
        for q in self.QUEUES:
            if q not in self.DMAQ:
                self.semh[q] = es.enter_context(nc.semaphore("s_" + q))
                self.cnt[q] = 0
        self.known = {self.stream(q): {} for q in self.QUEUES}
        self.items = {self.stream(q): [] for q in self.QUEUES}
        self._nds = 0

    def new_dmasem(self):
        name = "ds%d" % self._nds
        self._nds += 1
        self.semh[name] = self.es.enter_context(self.nc.semaphore(name))
        self.cnt[name] = 0
        return name

    def _wait(self, q, tok):
        if tok is None:
            return
        q = self.stream(q)
        pq, v = tok
        if v <= 0 or self.known[q].get(pq, 0) >= v:
            return
        sem = self.semh[pq]
        self.items[q].append(lambda e, sem=sem, v=v: e.wait_ge(sem, v))
        self.known[q][pq] = v

    def do(self, q, fns, reads=(), writes=(), extra_waits=()):
        if not isinstance(fns, (list, tuple)):
            fns = [fns]
        is_dma = q in self.DMAQ
        if is_dma:
            res0 = writes[0]
            if res0.dmasem is None:
                res0.dmasem = self.new_dmasem()
            sname = res0.dmasem
            step = 16
        else:
            sname = q
            step = 1
        for t in extra_waits:
            self._wait(q, t)
        for res in reads:
            self._wait(q, res.w)
        for res in writes:
            if res.w is not None and not (is_dma and res.w[0] == sname):
                self._wait(q, res.w)
            for pq, v in res.r.items():
                self._wait(q, (pq, v))
        sem = self.semh[sname]
        sq = self.stream(q)
        for fn in fns[:-1]:
            self.items[sq].append(lambda e, fn=fn: fn(e))
        last = fns[-1]

        def wrapped(e, fn=last, sem=sem, step=step):
            fn(e).then_inc(sem, step)
        self.items[sq].append(wrapped)
        self.cnt[sname] += step
        tok = (sname, self.cnt[sname])
        for res in reads:
            if res.r.get(q, 0) < self.cnt.get(q, 0) and not is_dma:
                pass
        if not is_dma:
            for res in reads:
                if res.r.get(sname, 0) < tok[1]:
                    res.r[sname] = tok[1]
        else:
            for res in reads:
                if res.r.get(sname, 0) < tok[1]:
                    res.r[sname] = tok[1]
        for res in writes:
            res.w = tok
            res.r = {}
        return tok

    def raw(self, q, fn):
        self.items[q].append(fn)


class Arena:
    def __init__(self, t):
        self.t = t

    def bf(self, off, n, p0=0, p1=128):
        return self.t[p0:p1, off:off + n]

    def f32(self, off, n, p0=0, p1=128):
        assert off % 2 == 0
        return self.t[:, :].bitcast(F32)[p0:p1, off // 2: off // 2 + n]


def build_program(cfg):
    c = derived(cfg)
    D, DC, HQ, HKV, G = c["D"], c["DC"], c["HQ"], c["HKV"], c["G"]
    OWN, EXT, WIN, KW = c["OWN"], c["EXT"], c["WIN"], c["KW"]
    NPASS, PT, QS, KT, OB = c["NPASS"], c["PT"], c["QS"], c["KT"], c["OB"]
    NSUB, MAXR, NG = c["NSUB"], c["MAXR"], c["NG"]
    GRP, HGRP = c["GRP"], c["HGRP"]
    HC = DC // 2

    nc = bass.Bass("TRN2", target_bir_lowering=False)

    def din(name, shape):
        return nc.dram_tensor(name, list(shape), F32, kind="ExternalInput").ap()

    xo = din("xo", (OWN, D))
    xh = din("xh", (WIN, D))
    wq_l = din("wq_l", (HQ, 128, DC * 128))
    wk_l = din("wk_l", (HKV, 128, DC * 128))
    wv_l = din("wv_l", (HKV, 128, DC * 128))
    wo_r = din("wo_r", (OB, HQ, 128, 512))
    W_r = din("W_r", (OB, DC, 128, 512))
    A_l = din("A_l", (128, DC * MAXR))
    B_l = din("B_l", (MAXR, D))
    g_l = din("g_l", (128, DC * NG))
    gb_c = din("gb_c", (NG, 1))
    b_row = din("b_row", (1, D))
    cos_e = din("cos_e", (128, EXT))
    sin_e = din("sin_e", (128, EXT))
    corr_l = din("corr_l", (128, NSUB))
    idnt = din("idnt", (128, 128))
    p128 = din("p128", (128, 128))
    m1t = din("m1t", (128, 128))
    m2t = din("m2t", (128, 128))
    thr = din("thr", (MAXR, 1))
    ones_r = din("ones_r", (1, 512))
    y = nc.dram_tensor("y", [OWN, D], F32, kind="ExternalOutput").ap()

    with contextlib.ExitStack() as es:
        E = es.enter_context
        cols_xg = DC * GRP
        cols_D = HQ * PT
        cols_xfm = DC * PT
        cols_krot = HKV * EXT
        cols_vtm = KT * HKV * 128
        cols_z1 = DC * PT
        B1 = E(nc.sbuf_tensor("B1", [128, max(cols_xg, cols_D, 4 * D)], BF16))
        B2 = E(nc.sbuf_tensor("B2", [128, max(cols_xg, cols_xfm, 4 * D,
                                              DC * MAXR + 3 * D + OWN)], BF16))
        B3 = E(nc.sbuf_tensor("B3", [128, max(cols_krot, cols_z1, 4 * D)], BF16))
        B4 = E(nc.sbuf_tensor("B4", [128, max(cols_vtm, 4 * D)], BF16))
        B5 = E(nc.sbuf_tensor("B5", [128, max(cols_z1, 4 * D, cols_D)], BF16))
        a1, a2, a3, a4, a5 = Arena(B1), Arena(B2), Arena(B3), Arena(B4), Arena(B5)

        c_cos = E(nc.sbuf_tensor("c_cos", [128, EXT], BF16))
        c_sin = E(nc.sbuf_tensor("c_sin", [128, EXT], BF16))
        c_id = E(nc.sbuf_tensor("c_id", [128, 128], BF16))
        c_idf = E(nc.sbuf_tensor("c_idf", [128, 128], F32))
        c_p128 = E(nc.sbuf_tensor("c_p128", [128, 128], BF16))
        c_m1 = E(nc.sbuf_tensor("c_m1", [128, 128], BF16))
        c_m2 = E(nc.sbuf_tensor("c_m2", [128, 128], BF16))
        c_ones = E(nc.sbuf_tensor("c_ones", [1, 512], BF16))
        c_thr = E(nc.sbuf_tensor("c_thr", [MAXR, 1], F32))
        c_corr = E(nc.sbuf_tensor("c_corr", [128, NSUB], F32))
        c_gbc = E(nc.sbuf_tensor("c_gbc", [NG, 1], F32))
        c_g = E(nc.sbuf_tensor("c_g", [128, DC * NG], BF16))

        SCR_N = 14720
        SCR = E(nc.sbuf_tensor("SCR", [128, SCR_N], BF16))
        scr = Arena(SCR)
        SLC_N = DC * 128
        SLC = (0, SLC_N)
        WRS = (2 * SLC_N, 2 * SLC_N + 512)
        base = 2 * SLC_N + 1024
        KRAW = base
        VRAW = KRAW + GRP
        SC1K = VRAW + GRP
        SC2K = SC1K + GRP
        assert SC2K + GRP <= SCR_N, SC2K + GRP
        QRAW = base
        QROT = QRAW + PT
        SC1 = QROT + PT
        SC2 = SC1 + PT
        PB = (SC2 + PT, SC2 + PT + KW)
        PTS = PB[1] + KW
        assert PTS + KW <= SCR_N, PTS + KW
        MSK = base
        Z1T = MSK + 512
        G8 = Z1T + 512
        GSB = G8 + 64
        IDXROW = GSB + 1024
        assert IDXROW + 512 <= SCR_N

        sums3 = E(nc.sbuf_tensor("sums3", [128, 3], F32))
        sumt = E(nc.sbuf_tensor("sumt", [128, 1], F32))
        rcp = E(nc.sbuf_tensor("rcp", [128, 1], F32))
        NSLOT = 4
        stats = E(nc.sbuf_tensor("stats", [128, 6 * (D // 512) * NSLOT], F32))
        mv = E(nc.sbuf_tensor("mv", [128, 2 * NSLOT], F32))
        rstd = E(nc.sbuf_tensor("rstd", [128, NSLOT], F32))
        nbias = E(nc.sbuf_tensor("nbias", [128, NSLOT], F32))
        gsb = scr.f32(GSB, 512, p0=0, p1=NG)
        mx8 = E(nc.sbuf_tensor("mx8", [128, 8], F32))
        idx8 = E(nc.sbuf_tensor("idx8", [128, 8], U32))
        idxf = E(nc.sbuf_tensor("idxf", [128, 1], F32))
        idxrow = scr.bf(IDXROW, PT, p0=0, p1=1)
        umsk = B2[0:MAXR, DC * MAXR + 3 * D: DC * MAXR + 3 * D + OWN]

        PS = E(nc.psum_tensor("PS", [128, 4096], F32))
        PSb = PS[:, :].bitcast(BF16)

        def bank(i):
            return PS[:, i * 512:(i + 1) * 512]

        def bankb(i):
            return PSb[:, i * 1024:(i + 1) * 1024]

        P = Prog(nc, es)

        r_const = Res()
        r_constS = Res()
        r_bank = [Res() for _ in range(8)]
        r_slc = [Res(), Res()]
        r_wrs = [Res(), Res()]
        r_xtm = [Res(), Res()]
        r_xg = [Res(), Res()]
        r_krot = Res()
        r_vtm = Res()
        r_kraw = Res()
        r_vraw = Res()
        r_sc12 = Res()
        r_xfm = Res()
        r_qraw = [Res(), Res()]
        r_qrot = [Res(), Res()]
        r_p = [Res(), Res()]
        r_pts = Res()
        r_D = Res()
        r_xres = Res()
        r_z1 = [Res(), Res()]
        r_z1tm = [Res(), Res()]
        r_sums = Res()
        r_sumt = Res()
        r_rcp = Res()
        r_stats = [Res() for _ in range(4)]
        r_mv = [Res() for _ in range(4)]
        r_rstd = [Res() for _ in range(4)]
        r_nbias = [Res() for _ in range(4)]
        r_gate = Res()
        r_idxrow = Res()
        r_umsk = Res()
        r_msk = Res()
        r_z2out = Res()
        r_y = [Res(), Res()]

        def ld_cast(dst, src):
            return lambda e: e.dma_start(out=dst, in_=src, max_dma_last_dim=8192)

        def ld_plain(dst, src):
            return lambda e: e.dma_start(out=dst, in_=src)

        for dst, src in [(c_id[:, :], idnt), (c_p128[:, :], p128), (c_m1[:, :], m1t),
                         (c_m2[:, :], m2t), (c_ones[:, :], ones_r),
                         (c_g[:, :], g_l), (c_cos[:, :], cos_e), (c_sin[:, :], sin_e)]:
            P.do("gdma", ld_cast(dst, src), writes=[r_const])
        for dst, src in [(c_idf[:, :], idnt), (c_thr[:, :], thr),
                         (c_corr[:, :], corr_l), (c_gbc[:, :], gb_c)]:
            P.do("sdma", ld_plain(dst, src), writes=[r_constS])
        RC = [r_const, r_constS]

        def xg_ap(g, kc, col0, ncol):
            buf = B2 if g % 2 == 0 else B1
            return buf[:, kc * GRP + col0: kc * GRP + col0 + ncol]

        def transpose_batches():
            out = []
            s = 0
            while s < DC:
                e_ = min(s + 8, DC)
                out.append(list(range(s, e_)))
                s = e_
            return out

        ntile_g = GRP // 128
        kv_bk = [0]

        def tile_stage(g, tt):
            ext_t = g * GRP + tt * 128
            if ext_t < WIN:
                tsrc = xh[ext_t:ext_t + 128, :]
            else:
                tsrc = xo[ext_t - WIN:ext_t - WIN + 128, :]
            sl = tt % 2
            xtm = a5.bf(sl * D, D)
            P.do("gdma", ld_cast(xtm, tsrc), writes=[r_xtm[sl]])
            xgbuf = B2 if g % 2 == 0 else B1
            for chunks in transpose_batches():
                bk = 5 + kv_bk[0] % 3
                kv_bk[0] += 1
                fns = []
                for j, dc in enumerate(chunks):
                    fns.append(lambda e, dc=dc, bk=bk, j=j, xtm=xtm: e.transpose(
                        bankb(bk)[:, j * 128:(j + 1) * 128],
                        xtm[:, dc * 128:(dc + 1) * 128], c_id[:, :]))
                P.do("pe", fns, reads=[r_xtm[sl]] + RC, writes=[r_bank[bk]])
                dc0 = chunks[0]
                dap = xgbuf[:, :].rearrange("p (c t) -> p c t", t=GRP)[
                    :, dc0:dc0 + len(chunks), tt * 128:(tt + 1) * 128]
                P.do("dve", lambda e, bk=bk, dap=dap, n=len(chunks): e.tensor_copy(
                    dap, bankb(bk)[:, 0:n * 128].rearrange("p (c t) -> p c t", t=128)),
                    reads=[r_bank[bk]], writes=[r_xg[g % 2]])

        for tt in range(ntile_g):
            tile_stage(0, tt)
        for g in range(HGRP):
            rxg = r_xg[g % 2]
            next_tiles = list(range(ntile_g)) if g + 1 < HGRP else []
            kv_tail = []
            for oc in range(2 * HKV + 1):
                if oc < 2 * HKV:
                    is_k = oc < HKV
                    hk = oc if is_k else oc - HKV
                    sl = oc % 2
                    pjb = oc % 2
                    wsl = scr.bf(SLC[sl], SLC_N)
                    P.do("gdma", ld_cast(wsl, (wk_l if is_k else wv_l)[hk]),
                         writes=[r_slc[sl]])
                    fns = []
                    for kc in range(DC):
                        fns.append(lambda e, kc=kc, wsl=wsl, pjb=pjb, g=g: e.matmul(
                            PS[:, pjb * 512: pjb * 512 + GRP], wsl[:, kc * 128:(kc + 1) * 128],
                            xg_ap(g, kc, 0, GRP),
                            start=(kc == 0), stop=(kc == DC - 1)))
                    P.do("pe", fns, reads=[r_slc[sl], rxg] + RC, writes=[r_bank[pjb]])
                step_t = max(1, (2 * HKV - 1) // ntile_g)
                if next_tiles and oc >= 2 and (oc - 2) % step_t == 0:
                    tile_stage(g + 1, next_tiles.pop(0))
                if next_tiles and oc == 2 * HKV:
                    while next_tiles:
                        tile_stage(g + 1, next_tiles.pop(0))
                if kv_tail:
                    kv_tail.pop(0)()
                if oc >= 2 * HKV:
                    continue
                if is_k:
                    kraw_e = scr.bf(KRAW, GRP)
                    P.do("act", lambda e, kraw=kraw_e, pjb=pjb: e.activation(
                        kraw, PS[:, pjb * 512: pjb * 512 + GRP], AF.Copy),
                        reads=[r_bank[pjb]], writes=[r_kraw])
                else:
                    vraw_e = scr.bf(VRAW, GRP)
                    P.do("act", lambda e, vraw=vraw_e, pjb=pjb: e.activation(
                        vraw, PS[:, pjb * 512: pjb * 512 + GRP], AF.Copy),
                        reads=[r_bank[pjb]], writes=[r_vraw])

                def kv_stage(is_k=is_k, hk=hk, pjb=pjb, g=g):
                    if is_k:
                        swb = 2 + hk % 2
                        kraw = scr.bf(KRAW, GRP)
                        P.do("pe", lambda e, kraw=kraw, swb=swb: e.matmul(
                            PS[:, swb * 512: swb * 512 + GRP], c_p128[:, :], kraw,
                            start=True, stop=True),
                            reads=[r_kraw] + RC, writes=[r_bank[swb]])
                        s1 = scr.bf(SC1K, GRP)
                        s2 = scr.bf(SC2K, GRP)
                        t_a = P.do("dve", lambda e, s1=s1, g=g, swb=swb: e.tensor_mul(
                            s1, PS[:, swb * 512: swb * 512 + GRP], c_sin[:, g * GRP:(g + 1) * GRP]),
                            reads=[r_bank[swb]] + RC, writes=[r_sc12])
                        t_b = P.do("dve", lambda e, s2=s2, kraw=kraw, g=g: e.tensor_mul(
                            s2, kraw, c_cos[:, g * GRP:(g + 1) * GRP]),
                            reads=[r_kraw] + RC, writes=[r_sc12])
                        P.do("dve", lambda e, s1=s1, s2=s2, hk=hk, g=g: e.tensor_add(
                            B3[:, hk * EXT + g * GRP: hk * EXT + (g + 1) * GRP], s1, s2),
                            reads=[r_sc12], writes=[r_krot], extra_waits=[t_a, t_b])
                    else:
                        vraw = scr.bf(VRAW, GRP)
                        fns = []
                        for tt in range(ntile_g):
                            fns.append(lambda e, tt=tt, vraw=vraw: e.transpose(
                                bankb(4)[:, tt * 128:(tt + 1) * 128],
                                vraw[:, tt * 128:(tt + 1) * 128], c_id[:, :]))
                        P.do("pe", fns, reads=[r_vraw] + RC, writes=[r_bank[4]])
                        vdst = B4[:, :].rearrange("p (kt h d) -> p kt h d", h=HKV, d=128)[
                            :, g * ntile_g:(g + 1) * ntile_g, hk, :]
                        P.do("dve", lambda e, vdst=vdst: e.tensor_copy(
                            vdst, bankb(4)[:, 0:ntile_g * 128].rearrange("p (t d) -> p t d", d=128)),
                            reads=[r_bank[4]], writes=[r_vtm])
                kv_tail.append(kv_stage)

        NST6 = 6 * (D // 512)

        def ln_chain(slot, zpre, rzpre, out, rout, extra_reads=(), skip_stats=False):
            st = stats[:, slot * NST6:(slot + 1) * NST6]
            mvs = mv[:, slot * 2:(slot + 1) * 2]
            rs = rstd[:, slot:slot + 1]
            nb = nbias[:, slot:slot + 1]
            nst = D // 512
            if not skip_stats:
                fns = []
                for q5 in range(nst):
                    fns.append(lambda e, q5=q5, zpre=zpre, st=st: e.bn_stats(
                        st[:, q5 * 6:(q5 + 1) * 6], zpre[:, q5 * 512:(q5 + 1) * 512]))
                P.do("dve", fns, reads=[rzpre], writes=[r_stats[slot]])
            P.do("dve", lambda e, st=st, mvs=mvs: e.bn_aggr(mvs, st),
                 reads=[r_stats[slot]], writes=[r_mv[slot]])
            P.do("dve", lambda e, mvs=mvs, rs=rs: e.tensor_scalar_add(
                rs, mvs[:, 1:2], float(c["EPS"])),
                reads=[r_mv[slot]], writes=[r_rstd[slot]])
            P.do("dve", lambda e, rs=rs: e.reciprocal(rs, rs),
                 reads=[r_rstd[slot]], writes=[r_rstd[slot]])
            P.do("act", lambda e, rs=rs: e.activation(rs, rs, AF.Sqrt),
                 reads=[r_rstd[slot]], writes=[r_rstd[slot]])
            P.do("dve", lambda e, mvs=mvs, rs=rs, nb=nb: e.tensor_mul(nb, mvs[:, 0:1], rs),
                 reads=[r_mv[slot], r_rstd[slot]], writes=[r_nbias[slot]])
            P.do("dve", lambda e, nb=nb: e.tensor_scalar_mul(nb, nb, -1.0),
                 reads=[r_nbias[slot]], writes=[r_nbias[slot]])
            P.do("act", lambda e, out=out, zpre=zpre, rs=rs, nb=nb: e.activation(
                out, zpre, AF.Identity, bias=nb, scale=rs),
                reads=[rzpre, r_rstd[slot], r_nbias[slot]] + list(extra_reads),
                writes=[rout])

        sm = c["SM"]
        vview = B4[:, :].rearrange("p (kt h d) -> p kt h d", h=HKV, d=128)
        for r in (r_qraw[0], r_qraw[1], r_qrot[0], r_qrot[1], r_p[0], r_p[1],
                  r_pts, r_wrs[0], r_wrs[1]):
            r.take(r_kraw)
            r.take(r_vraw)
            r.take(r_sc12)
        NOREBUILD = (PT == GRP and WIN % GRP == 0 and not cfg.get("FORCE_REBUILD"))
        for p_i in range(NPASS):
            tok0 = p_i * PT
            g_pass = (WIN + tok0) // GRP
            if NOREBUILD:
                xgb = B2 if g_pass % 2 == 0 else B1
                rxfm_r = r_xg[g_pass % 2]

                def xfm_ap(kc, xgb=xgb):
                    return xgb[:, kc * GRP: kc * GRP + PT]
                XR = Arena(xgb)
                if p_i == 0:
                    DBt = B5
                    r_D.take(r_xtm[0])
                    r_D.take(r_xtm[1])
                else:
                    g_prev = (WIN + (p_i - 1) * PT) // GRP
                    DBt = B2 if g_prev % 2 == 0 else B1
                    r_D.take(r_xres)
                    r_D.take(r_xg[g_prev % 2])
            else:
                rxfm_r = r_xfm

                def xfm_ap(kc):
                    return B2[:, kc * PT: (kc + 1) * PT]
                XR = a2
                DBt = B1
                r_xtm[0].take(r_D)
                r_xtm[1].take(r_D)
                for rr in (r_xtm[0], r_xtm[1]):
                    rr.take(r_xg[0])
                    rr.take(r_xg[1])
                    rr.take(r_z1tm[0])
                    rr.take(r_z1tm[1])
                r_xfm.take(r_xres)
                r_xfm.take(r_xg[0])
                r_xfm.take(r_xg[1])
                xa = a5 if p_i == 0 else a1
                for ts in range(QS):
                    sl = ts % 2
                    xtm = xa.bf(sl * D, D)
                    P.do("gdma", ld_cast(xtm, xo[tok0 + ts * 128: tok0 + (ts + 1) * 128, :]),
                         writes=[r_xtm[sl]])
                    for bi, chunks in enumerate(transpose_batches()):
                        bk = (ts % 2) * 4 + bi % 4
                        fns = []
                        for j, dc in enumerate(chunks):
                            fns.append(lambda e, dc=dc, bk=bk, j=j, xtm=xtm: e.transpose(
                                bankb(bk)[:, j * 128:(j + 1) * 128],
                                xtm[:, dc * 128:(dc + 1) * 128], c_id[:, :]))
                        P.do("pe", fns, reads=[r_xtm[sl]] + RC, writes=[r_bank[bk]])
                        dc0 = chunks[0]
                        dap = B2[:, :].rearrange("p (c t) -> p c t", t=PT)[
                            :, dc0:dc0 + len(chunks), ts * 128:(ts + 1) * 128]
                        P.do("dve", lambda e, bk=bk, dap=dap, n=len(chunks): e.tensor_copy(
                            dap, bankb(bk)[:, 0:n * 128].rearrange("p (c t) -> p c t", t=128)),
                            reads=[r_bank[bk]], writes=[r_xfm])
                r_D.take(r_xtm[0])
                r_D.take(r_xtm[1])
            tailsT = []
            tailsA = []
            qsc = [0]

            QR = (QRAW, 2 * SLC_N)
            QT = (QROT, 2 * SLC_N + 512)

            def emit_A_load(h):
                sl = h % 2
                wsl = scr.bf(SLC[sl], SLC_N)
                P.do("gdma", ld_cast(wsl, wq_l[h]), writes=[r_slc[sl]])

            def emit_A_piece(h, j, npiece):
                sl = h % 2
                wsl = scr.bf(SLC[sl], SLC_N)
                per = DC // npiece
                fns = []
                for kc in range(j * per, (j + 1) * per):
                    fns.append(lambda e, kc=kc, wsl=wsl, xfm_ap=xfm_ap: e.matmul(
                        PS[:, 7 * 512: 7 * 512 + PT], wsl[:, kc * 128:(kc + 1) * 128],
                        xfm_ap(kc),
                        start=(kc == 0), stop=(kc == DC - 1)))
                P.do("pe", fns, reads=[r_slc[sl], rxfm_r],
                     writes=[r_bank[7]] if j in (0, npiece - 1) else [])

            def emit_B1(h):
                hp = h % 2
                qraw = scr.bf(QR[hp], PT)
                P.do("act", lambda e, qraw=qraw: e.activation(
                    qraw, PS[:, 7 * 512: 7 * 512 + PT], AF.Copy, scale=float(sm)),
                    reads=[r_bank[7]], writes=[r_qraw[hp]])

            def emit_B2(h):
                hp = h % 2
                qraw = scr.bf(QR[hp], PT)
                qrot = scr.bf(QT[hp], PT)
                P.do("pe", lambda e, qraw=qraw: e.matmul(
                    PS[:, 7 * 512: 7 * 512 + PT], c_p128[:, :], qraw,
                    start=True, stop=True),
                    reads=[r_qraw[hp]] + RC, writes=[r_bank[7]])
                s1 = scr.bf(SC1, PT)
                s2 = scr.bf(SC2, PT)
                cs0 = WIN + tok0
                t_a = P.do("dve", lambda e, s1=s1, cs0=cs0: e.tensor_mul(
                    s1, PS[:, 7 * 512: 7 * 512 + PT], c_sin[:, cs0:cs0 + PT]),
                    reads=[r_bank[7]] + RC, writes=[r_sc12])
                t_b = P.do("dve", lambda e, s2=s2, qraw=qraw, cs0=cs0: e.tensor_mul(
                    s2, qraw, c_cos[:, cs0:cs0 + PT]),
                    reads=[r_qraw[hp]] + RC, writes=[r_sc12])
                P.do("dve", lambda e, qrot=qrot, s1=s1, s2=s2: e.tensor_add(qrot, s1, s2),
                     reads=[r_sc12], writes=[r_qrot[hp]], extra_waits=[t_a, t_b])

            def emit_C(h, s4):
                hk = h // G
                hp = h % 2
                qrot = scr.bf(QT[hp], PT)
                sg = p_i * QS + s4
                elo = sg * 128
                par = qsc[0] % 2
                qsc[0] += 1
                bs = 0 if par == 0 else 3
                pb = scr.bf(PB[par], KW)
                rp = r_p[par]
                schunks = []
                off = 0
                while off < KW:
                    wdt = min(512, KW - off)
                    schunks.append((off, wdt))
                    off += wdt
                nch = len(schunks)
                av_bk = (3 - bs) + nch - 1
                t9_bk = bs + nch - 1
                av_off = schunks[-1][1]
                n_extra = max(0, KW // 128 - 8)
                assert 2 * (av_off + 128) + n_extra * 128 <= 1024
                sbanks = [r_bank[bs + ci] for ci in range(nch)]
                for ci, (off, wdt) in enumerate(schunks):
                    bkk = bs + ci
                    has_m1 = off == 0
                    has_m2 = off + wdt == KW
                    fns = [lambda e, off=off, wdt=wdt, bkk=bkk, s4=s4, hk=hk, elo=elo, qrot=qrot, st=(not has_m1 and not has_m2):
                           e.matmul(PS[:, bkk * 512: bkk * 512 + wdt],
                                    qrot[:, s4 * 128:(s4 + 1) * 128],
                                    B3[:, hk * EXT + elo + off: hk * EXT + elo + off + wdt],
                                    start=True, stop=st)]
                    if has_m1:
                        fns.append(lambda e, bkk=bkk, st=(not has_m2): e.matmul(
                            PS[:, bkk * 512: bkk * 512 + 128], c_id[:, :], c_m1[:, :],
                            start=False, stop=st))
                    if has_m2:
                        fns.append(lambda e, bkk=bkk, off=off, wdt=wdt: e.matmul(
                            PS[:, bkk * 512 + wdt - 128: bkk * 512 + wdt],
                            c_id[:, :], c_m2[:, :], start=False, stop=True))
                    P.do("pe", fns, reads=[r_qrot[hp], r_krot] + RC, writes=[r_bank[bkk]])

                def emit_post(bs=bs, pb=pb, rp=rp, sg=sg, sbanks=sbanks):
                    P.do("act", lambda e, bs=bs, pb=pb: e.activation(
                        pb[:, 0:KW], PS[:, bs * 512: bs * 512 + KW],
                        AF.Exp, accum_out=sums3[:, 0:1]),
                        reads=sbanks, writes=[rp, r_sums])
                    P.do("dve", lambda e, sg=sg: e.tensor_add(
                        sumt[:, :], sums3[:, 0:1], c_corr[:, sg:sg + 1]),
                        reads=[r_sums] + RC, writes=[r_sumt])
                    P.do("dve", lambda e: e.reciprocal(rcp[:, :], sumt[:, :]),
                         reads=[r_sumt], writes=[r_rcp])
                    P.do("dve", lambda e, pb=pb: e.tensor_scalar_mul(pb, pb, rcp[:, :]),
                         reads=[rp, r_rcp], writes=[rp])

                def tailT(h=h, s4=s4, sg=sg, hk=hk, pb=pb, rp=rp,
                          av_bk=t9_bk, av_off=av_off):
                    nfull = KW // 128
                    pts = scr.bf(PTS, KW)
                    n6 = min(8, nfull)
                    fns = []
                    for j in range(n6):
                        fns.append(lambda e, j=j, pb=pb: e.transpose(
                            bankb(6)[:, j * 128:(j + 1) * 128],
                            pb[:, j * 128:(j + 1) * 128], c_id[:, :]))
                    extra = []
                    for j in range(n6, nfull):
                        xoff = 2 * (av_off + 128) + (j - n6) * 128
                        extra.append(xoff)
                        fns.append(lambda e, j=j, pb=pb, xoff=xoff, av_bk=av_bk: e.transpose(
                            bankb(av_bk)[:, xoff:xoff + 128],
                            pb[:, j * 128:(j + 1) * 128], c_id[:, :]))
                    P.do("pe", fns, reads=[rp] + RC, writes=[r_bank[6], r_bank[av_bk]])
                    P.do("dve", lambda e, pts=pts, n6=n6: e.tensor_copy(
                        pts[:, 0:n6 * 128], bankb(6)[:, 0:n6 * 128]),
                        reads=[r_bank[6]], writes=[r_pts])
                    if nfull > n6:
                        P.do("dve", lambda e, pts=pts, n6=n6, nfull=nfull, av_bk=av_bk,
                             x0=extra[0]: e.tensor_copy(
                            pts[:, n6 * 128:nfull * 128],
                            bankb(av_bk)[:, x0:x0 + (nfull - n6) * 128]),
                            reads=[r_bank[av_bk]], writes=[r_pts])

                def tailA(h=h, s4=s4, sg=sg, hk=hk,
                          av_bk=av_bk, av_off=av_off):
                    nfull = KW // 128
                    pts = scr.bf(PTS, KW)
                    fns = []
                    for j in range(nfull):
                        kt = sg + j
                        fns.append(lambda e, j=j, kt=kt, hk=hk, pts=pts:
                                   e.matmul(PS[:, av_bk * 512 + av_off: av_bk * 512 + av_off + 128],
                                            vview[:, kt, hk, :],
                                            pts[:, j * 128:(j + 1) * 128],
                                            start=(j == 0), stop=(j == nfull - 1)))
                    P.do("pe", fns, reads=[r_pts, r_vtm], writes=[r_bank[av_bk]])
                    P.do("act", lambda e, DBt=DBt: e.activation(
                        DBt[:, h * PT + s4 * 128: h * PT + (s4 + 1) * 128],
                        PS[:, av_bk * 512 + av_off: av_bk * 512 + av_off + 128],
                        AF.Copy),
                        reads=[r_bank[av_bk]], writes=[r_D])
                return emit_post, tailT, tailA

            emit_A_load(0)
            for j in range(QS):
                emit_A_piece(0, j, QS)
            emit_B1(0)
            emit_B2(0)
            for h in range(HQ):
                if h + 1 < HQ:
                    emit_A_load(h + 1)
                for s4 in range(QS):
                    post, tT, tA = emit_C(h, s4)
                    post()
                    if len(tailsA) >= 2:
                        tailsA.pop(0)()
                    if h + 1 < HQ:
                        emit_A_piece(h + 1, s4, QS)
                    if s4 == QS - 1 and h + 1 < HQ:
                        emit_B1(h + 1)
                    if tailsT:
                        tailsT.pop(0)()
                    tailsT.append(tT)
                    tailsA.append(tA)
                    if s4 == QS - 1 and h + 1 < HQ:
                        emit_B2(h + 1)
            while tailsT or tailsA:
                if tailsA:
                    tailsA.pop(0)()
                if tailsT:
                    tailsT.pop(0)()
            if cfg.get("DEBUG_D") and p_i == 0:
                P.do("gdma", lambda e, DBt=DBt: e.dma_start(
                    out=y[0:128, :].rearrange("p (a b) -> p a b", a=2) if False else y[0:256, :].rearrange("(a p) m -> p a m", p=128),
                    in_=DBt[:, 0:HQ * PT].rearrange("p (a m) -> p a m", a=2),
                    max_dma_last_dim=8192),
                    reads=[r_D], writes=[r_y[0]])
            r_xres.take(rxfm_r)
            for ts_ in range(QS):
                P.do("gdma", ld_cast(XR.bf(ts_ * D, D),
                                     xo[tok0 + ts_ * 128: tok0 + (ts_ + 1) * 128, :]),
                     writes=[r_xres])
            ch_sz = max(1, min(8, HQ, SLC_N // 512))
            n_ch = HQ // ch_sz
            for ob in range(OB):
                lastw = None
                for ch in range(n_ch):
                    wch = scr.bf(SLC[ch % 2], ch_sz * 512)
                    P.do("gdma", lambda e, wch=wch, ob=ob, ch=ch: e.dma_start(
                        out=wch[:, :].rearrange("p (s m) -> p s m", m=512),
                        in_=wo_r[ob, ch * ch_sz:(ch + 1) * ch_sz].rearrange("s p m -> p s m"),
                        max_dma_last_dim=8192),
                        writes=[r_slc[ch % 2]])
                    for kcl in range(ch_sz):
                        kc = ch * ch_sz + kcl
                        fns = []
                        for ts in range(QS):
                            fns.append(lambda e, ts=ts, kc=kc, wch=wch, kcl=kcl, DBt=DBt: e.matmul(
                                bank(ts), DBt[:, kc * PT + ts * 128: kc * PT + (ts + 1) * 128],
                                wch[:, kcl * 512:(kcl + 1) * 512],
                                start=(kc == 0), stop=(kc == HQ - 1)))
                        wset = [r_bank[t] for t in range(QS)] if kc in (0, HQ - 1) else []
                        lastw = P.do("pe", fns, reads=[r_slc[ch % 2], r_D], writes=wset)
                for ts in range(QS):
                    xres_sl = XR.bf(ts * D + ob * 512, 512)
                    P.do("dve", lambda e, ts=ts, xres_sl=xres_sl: e.tensor_add(
                        xres_sl, bank(ts), xres_sl),
                        reads=[r_bank[ts], r_xres], writes=[r_xres],
                        extra_waits=[lastw])
                    slot = ts % NSLOT
                    P.do("dve", lambda e, xres_sl=xres_sl, slot=slot, ob=ob: e.bn_stats(
                        stats[:, slot * NST6 + ob * 6: slot * NST6 + (ob + 1) * 6], xres_sl),
                        reads=[r_xres], writes=[r_stats[slot]])
            z1buf = B5 if p_i == 0 else B3
            rz1 = r_z1[p_i]
            if p_i == 0:
                rz1.take(r_xtm[0])
                rz1.take(r_xtm[1])
            else:
                rz1.take(r_krot)
            if NOREBUILD and p_i == 0:
                rz1.take(r_D)
            DBGPRE_ONLY = cfg.get("DEBUG_PRE") and not cfg.get("DEBUG_Z1")
            if not NOREBUILD:
                r_z1tm[0].take(r_D)
                r_z1tm[1].take(r_D)
            for ts in range(QS):
                zpre = XR.bf(ts * D, D)
                if NOREBUILD:
                    z1tm = zpre
                    rzt = r_xres
                else:
                    z1tm = a1.bf((ts % 2) * D, D)
                    rzt = r_z1tm[ts % 2]
                if cfg.get("DEBUG_PRE"):
                    P.do("gdma", lambda e, zpre=zpre, tok0=tok0, ts=ts: e.dma_start(
                        out=y[tok0 + ts * 128: tok0 + (ts + 1) * 128, :],
                        in_=zpre, max_dma_last_dim=8192),
                        reads=[r_xres], writes=[r_y[0]])
                ln_chain(ts % NSLOT, zpre, r_xres, z1tm, rzt, skip_stats=True)
                if cfg.get("DEBUG_Z1"):
                    P.do("gdma", lambda e, z1tm=z1tm, tok0=tok0, ts=ts: e.dma_start(
                        out=y[tok0 + ts * 128: tok0 + (ts + 1) * 128, :],
                        in_=z1tm, max_dma_last_dim=8192),
                        reads=[rzt], writes=[r_y[0]])
                for bi, chunks in enumerate(transpose_batches()):
                    bk = 4 + ((ts % 2) * 2 + bi) % 4
                    fns = []
                    for j, dc in enumerate(chunks):
                        fns.append(lambda e, dc=dc, bk=bk, j=j, z1tm=z1tm: e.transpose(
                            bankb(bk)[:, j * 128:(j + 1) * 128],
                            z1tm[:, dc * 128:(dc + 1) * 128], c_id[:, :]))
                    P.do("pe", fns, reads=[rzt] + RC, writes=[r_bank[bk]])
                    dc0 = chunks[0]
                    dap = z1buf[:, :].rearrange("p (c t) -> p c t", t=PT)[
                        :, dc0:dc0 + len(chunks), ts * 128:(ts + 1) * 128]
                    P.do("dve", lambda e, bk=bk, dap=dap, n=len(chunks): e.tensor_copy(
                        dap, bankb(bk)[:, 0:n * 128].rearrange("p (c t) -> p c t", t=128)),
                        reads=[r_bank[bk]], writes=[rz1])

        if cfg.get("DEBUG_Z1") or cfg.get("DEBUG_PRE"):
            for ry in r_y:
                if ry.dmasem is not None:
                    ysem = P.semh[ry.dmasem]
                    P.raw("sdma", lambda e, ysem=ysem, v=P.cnt[ry.dmasem]: e.wait_ge(ysem, v))
            return _finish(nc, P, es)
        cA = a2.bf(0, DC * MAXR)
        cB_off = DC * MAXR
        cB = a2.bf(cB_off, D, p0=0, p1=MAXR)
        r_cA = Res()
        r_cA.take(r_xres)
        r_cA.take(r_D)
        r_cB = Res()
        r_cB.take(r_xres)
        r_cB.take(r_D)
        P.do("gdma", ld_cast(cA, A_l), writes=[r_cA])
        P.do("gdma", ld_cast(cB, B_l), writes=[r_cB])

        def z1_ap(dc, col0, ncol):
            buf = B5 if col0 < PT else B3
            base = col0 if col0 < PT else col0 - PT
            return buf[:, dc * PT + base: dc * PT + base + ncol]

        c_brow = B2[0:1, cB_off + D: cB_off + 2 * D]
        r_brow = Res()
        r_brow.take(r_xres)
        r_brow.take(r_D)
        for rr in (r_msk, r_sc12):
            for src_r in (r_slc[0], r_slc[1], r_qraw[0], r_qraw[1], r_qrot[0],
                          r_qrot[1], r_p[0], r_p[1], r_pts, r_kraw, r_vraw):
                rr.take(src_r)
        r_msk.take(r_sc12)
        r_umsk.take(r_D)
        r_umsk.take(r_xres)
        P.do("gdma", ld_cast(c_brow, b_row), writes=[r_brow])
        for th in range(NPASS):
            rz = r_z1[th]
            fns = []
            for dc in range(DC):
                fns.append(lambda e, dc=dc, th=th: e.matmul(
                    PS[0:MAXR, 0:PT], cA[:, dc * MAXR:(dc + 1) * MAXR],
                    z1_ap(dc, th * PT, PT), start=(dc == 0), stop=(dc == DC - 1)))
            P.do("pe", fns, reads=[r_cA, rz], writes=[r_bank[0]])
            fns = []
            for dc in range(DC):
                fns.append(lambda e, dc=dc, th=th: e.matmul(
                    PS[0:NG, 512:512 + PT], c_g[:, dc * NG:(dc + 1) * NG],
                    z1_ap(dc, th * PT, PT), start=(dc == 0), stop=(dc == DC - 1)))
            P.do("pe", fns, reads=[rz] + RC, writes=[r_bank[1]])
            P.do("act", lambda e: e.activation(gsb[:, 0:PT], PS[0:NG, 512:512 + PT],
                                               AF.Identity, bias=c_gbc[:, :]),
                 reads=[r_bank[1]] + RC, writes=[r_gate])
            for ts in range(QS):
                P.do("pe", lambda e, ts=ts: e.transpose(
                    PS[0:128, 1024 + ts * 4: 1024 + (ts + 1) * 4],
                    gsb[:, ts * 128:(ts + 1) * 128], c_idf[0:NG, 0:NG]),
                    reads=[r_gate] + RC, writes=[r_bank[2]])
                g8 = scr.f32(G8, 8)
                P.do("dve", lambda e, g8=g8: e.memset(g8, -1e30), writes=[r_msk])
                P.do("dve", lambda e, ts=ts, g8=g8: e.tensor_copy(
                    g8[:, 0:NG], PS[0:128, 1024 + ts * 4:1024 + (ts + 1) * 4]),
                    reads=[r_bank[2], r_msk], writes=[r_msk])
                P.do("dve", lambda e, g8=g8: e.max(mx8[:, :], g8),
                     reads=[r_msk], writes=[r_gate])
                P.do("dve", lambda e, g8=g8: e.max_index(idx8[:, :], mx8[:, :], g8),
                     reads=[r_gate, r_msk], writes=[r_gate])
                P.do("dve", lambda e: e.tensor_copy(idxf[:, :], idx8[:, 0:1]),
                     reads=[r_gate], writes=[r_gate])
                P.do("pe", lambda e, ts=ts: e.transpose(
                    PS[0:1, 1536 + ts * 128: 1536 + (ts + 1) * 128], idxf[:, :], c_idf[:, :]),
                    reads=[r_gate] + RC, writes=[r_bank[3]])
                P.do("dve", lambda e, ts=ts: e.tensor_copy(
                    idxrow[:, ts * 128:(ts + 1) * 128],
                    PS[0:1, 1536 + ts * 128:1536 + (ts + 1) * 128]),
                    reads=[r_bank[3]], writes=[r_idxrow])
            P.do("pe", lambda e: e.matmul(PS[0:MAXR, 1024:1024 + PT],
                                          c_ones[:, 0:MAXR], idxrow[:, 0:PT],
                                          start=True, stop=True),
                 reads=[r_idxrow] + RC, writes=[r_bank[2]])
            msk = scr.bf(MSK, 512, p0=0, p1=MAXR)
            P.do("dve", lambda e, msk=msk: e.tensor_scalar(
                msk[:, 0:PT], PS[0:MAXR, 1024:1024 + PT], c_thr[:, :], None, ALU.is_ge),
                reads=[r_bank[2]] + RC, writes=[r_msk])
            P.do("dve", lambda e, th=th, msk=msk: e.tensor_mul(
                umsk[:, th * PT:(th + 1) * PT], PS[0:MAXR, 0:PT], msk[:, 0:PT]),
                reads=[r_bank[0], r_msk], writes=[r_umsk])

        last_g = (WIN + (NPASS - 1) * PT) // GRP
        INPLACE_T8 = (NSUB - QS) if (NOREBUILD and last_g % 2 == 1) else NSUB

        def z2pre_ap(t8, col0, ncol):
            if NOREBUILD:
                if t8 < INPLACE_T8:
                    return a4.bf(t8 * D + col0, ncol)
                return a1.bf((t8 - INPLACE_T8) * D + col0, ncol)
            if t8 < 4:
                return a1.bf(t8 * D + col0, ncol)
            if t8 < 7:
                return a4.bf((t8 - 3) * D + col0, ncol)
            return a2.bf(cB_off + 2 * D + col0, ncol)

        r_z2 = [Res() for _ in range(NSUB)]
        for t8 in range(NSUB):
            if NOREBUILD:
                if t8 < INPLACE_T8:
                    r_z2[t8].take(r_vtm)
                else:
                    r_z2[t8].take(r_xres)
            elif t8 < 4:
                r_z2[t8].take(r_D)
                r_z2[t8].take(r_xres)
                r_z2[t8].take(r_z1tm[0])
                r_z2[t8].take(r_z1tm[1])
            elif t8 < 7:
                r_z2[t8].take(r_vtm)
            else:
                r_z2[t8].take(r_cB)
        r_z2out2 = [r_z2out, Res()]
        r_z2out2[1].take(r_z2out)

        def emit_ln2(t8):
            zpre = z2pre_ap(t8, 0, D)
            rzo = r_z2out2[t8 % 2]
            if t8 < PT // 128:
                rzo.take(r_z1[0])
                zout = a5.f32((t8 % 2) * 2 * D, D)
            else:
                rzo.take(r_z1[1])
                zout = a3.f32((t8 % 2) * 2 * D, D)
            ln_chain(t8 % NSLOT, zpre, r_z2[t8], zout, rzo, skip_stats=True)
            P.do("sdma", lambda e, t8=t8, zout=zout: e.dma_start(
                out=y[t8 * 128:(t8 + 1) * 128, :], in_=zout),
                reads=[rzo], writes=[r_y[t8 % 2]])

        waves = [list(range(s, min(s + 4, NSUB))) for s in range(0, NSUB, 4)]
        ch_f = max(1, min(8, DC, SLC_N // 512))
        n_chf = DC // ch_f
        for wave in waves:
            for ob in range(OB):
                for ch in range(n_chf):
                    wch = scr.bf(SLC[ch % 2], ch_f * 512)
                    P.do("gdma", lambda e, wch=wch, ob=ob, ch=ch: e.dma_start(
                        out=wch[:, :].rearrange("p (s m) -> p s m", m=512),
                        in_=W_r[ob, ch * ch_f:(ch + 1) * ch_f].rearrange("s p m -> p s m"),
                        max_dma_last_dim=8192),
                        writes=[r_slc[ch % 2]])
                    for dcl in range(ch_f):
                        dc = ch * ch_f + dcl
                        fns = []
                        for tj, t8 in enumerate(wave):
                            fns.append(lambda e, tj=tj, t8=t8, dc=dc, wch=wch, dcl=dcl: e.matmul(
                                bank(tj), z1_ap(dc, t8 * 128, 128),
                                wch[:, dcl * 512:(dcl + 1) * 512],
                                start=(dc == 0), stop=False))
                        P.do("pe", fns, reads=[r_slc[ch % 2], r_z1[0], r_z1[1]],
                             writes=[r_bank[tj] for tj in range(len(wave))] if dc == 0 else [])
                fns = []
                for tj, t8 in enumerate(wave):
                    fns.append(lambda e, tj=tj, t8=t8, ob=ob: e.matmul(
                        bank(tj), umsk[:, t8 * 128:(t8 + 1) * 128],
                        cB[:, ob * 512:(ob + 1) * 512], start=False, stop=False))
                    fns.append(lambda e, tj=tj, ob=ob: e.matmul(
                        bank(tj), c_ones[:, 0:128], c_brow[:, ob * 512:(ob + 1) * 512],
                        start=False, stop=True))
                tk_g = P.do("pe", fns, reads=[r_umsk, r_cB, r_brow] + RC,
                            writes=[r_bank[tj] for tj in range(len(wave))])
                for tj, t8 in enumerate(wave):
                    noc = min(4, DC - ob * 4)
                    if NOREBUILD and t8 >= INPLACE_T8:
                        P.do("dve", lambda e, tj=tj, t8=t8, ob=ob, noc=noc: e.tensor_add(
                            z2pre_ap(t8, ob * 512, noc * 128),
                            PS[:, tj * 512: tj * 512 + noc * 128],
                            z2pre_ap(t8, ob * 512, noc * 128)),
                            reads=[r_bank[tj], r_xres], writes=[r_z2[t8]],
                            extra_waits=[tk_g])
                    else:
                        fns = []
                        for j in range(noc):
                            oc = ob * 4 + j
                            fns.append(lambda e, j=j, oc=oc, t8=t8: e.transpose(
                                bankb(4)[:, j * 128:(j + 1) * 128],
                                z1_ap(oc, t8 * 128, 128), c_id[:, :]))
                        P.do("pe", fns, reads=[r_z1[0], r_z1[1]] + RC, writes=[r_bank[4]],
                             extra_waits=[tk_g])
                        z1t = scr.bf(Z1T, 512)
                        P.do("dve", lambda e, z1t=z1t, noc=noc: e.tensor_copy(
                            z1t[:, 0:noc * 128], bankb(4)[:, 0:noc * 128]),
                            reads=[r_bank[4]], writes=[r_sc12])
                        P.do("dve", lambda e, tj=tj, t8=t8, ob=ob, z1t=z1t, noc=noc: e.tensor_add(
                            z2pre_ap(t8, ob * 512, noc * 128), PS[:, tj * 512: tj * 512 + noc * 128],
                            z1t[:, 0:noc * 128]),
                            reads=[r_bank[tj], r_sc12], writes=[r_z2[t8]],
                            extra_waits=[tk_g])
                    slot = t8 % NSLOT
                    P.do("dve", lambda e, t8=t8, slot=slot, ob=ob: e.bn_stats(
                        stats[:, slot * NST6 + ob * 6: slot * NST6 + (ob + 1) * 6],
                        z2pre_ap(t8, ob * 512, 512)),
                        reads=[r_z2[t8]], writes=[r_stats[slot]])
            if ob == OB - 1:
                for t8w in wave:
                    emit_ln2(t8w)

        for ry in r_y:
            ysem = P.semh[ry.dmasem]
            final_cnt = P.cnt[ry.dmasem]
            P.raw("sdma", lambda e, ysem=ysem, v=final_cnt: e.wait_ge(ysem, v))
        return _finish(nc, P, es)


def _finish(nc, P, es):
    if True:
        block = es.enter_context(nc.Block())

        @block.tensor
        def _(t):
            for fn in P.items["pe"]:
                fn(t)

        @block.scalar
        def _(sc):
            for fn in P.items["act"]:
                fn(sc)

        @block.vector
        def _(v):
            for fn in P.items["dve"]:
                fn(v)

        @block.gpsimd
        def _(g):
            for fn in P.items["gdma"]:
                fn(g)

        @block.sync
        def _(s):
            for fn in P.items["sdma"]:
                fn(s)
    return nc


def host_constants(cfg):
    c = derived(cfg)
    MAXR = c["MAXR"]
    out = {}
    out["idnt"] = np.eye(128, dtype=np.float32)
    pm = np.zeros((128, 128), dtype=np.float32)
    half = c["HD"] // 2
    for i in range(128):
        pm[i, (i + half) % c["HD"]] = 1.0
    out["p128"] = pm
    BIG = -30720.0
    i_ = np.arange(128)[:, None]
    j_ = np.arange(128)[None, :]
    out["m1t"] = np.where(j_ > i_, 0.0, BIG).astype(np.float32)
    out["m2t"] = np.where(j_ <= i_, 0.0, BIG).astype(np.float32)
    th = np.zeros((MAXR, 1), dtype=np.float32)
    for j in range(MAXR):
        k = 0
        while j >= cfg["RANKS"][k]:
            k += 1
        th[j, 0] = k - 0.5
    out["thr"] = th
    out["ones_r"] = np.ones((1, 512), dtype=np.float32)
    return out


def host_percore(cfg, core, x_full):
    c = derived(cfg)
    D, OWN, WIN, NSUB, HD = c["D"], c["OWN"], c["WIN"], c["NSUB"], c["HD"]
    cb, ci = core // c["NCORE_PER_B"], core % c["NCORE_PER_B"]
    t0 = ci * OWN
    xo = np.ascontiguousarray(x_full[cb, t0:t0 + OWN, :]).astype(np.float32)
    if ci == 0:
        xhalo = np.zeros((WIN, D), dtype=np.float32)
    else:
        xhalo = np.ascontiguousarray(x_full[cb, t0 - WIN:t0, :]).astype(np.float32)
    pos = np.arange(t0 - WIN, t0 + OWN, dtype=np.float64)
    pos = np.clip(pos, 0, None)
    half = HD // 2
    inv = cfg["THETA"] ** (-np.arange(half, dtype=np.float64) / half)
    ang = inv[:, None] * pos[None, :]
    cos = np.cos(ang)
    sin = np.sin(ang)
    cos_e = np.concatenate([cos, cos], axis=0).astype(np.float32)
    sin_e = np.concatenate([-sin, sin], axis=0).astype(np.float32)
    corr = np.zeros((128, NSUB), dtype=np.float32)
    if ci == 0:
        ir = np.arange(128)
        for sg in range(NSUB):
            L = WIN - 128 * sg
            tri = np.clip(np.minimum(128, L) - ir - 1, 0, None)
            mid = max(0, min(WIN, L) - 128)
            corr[:, sg] = -(tri + mid)
    return dict(xo=xo, xh=xhalo, cos_e=cos_e, sin_e=sin_e, corr_l=corr)


def host_weights(cfg, W):
    c = derived(cfg)
    D, DC, HQ, HKV, OB = c["D"], c["DC"], c["HQ"], c["HKV"], c["OB"]
    MAXR, NG = c["MAXR"], c["NG"]
    out = {}

    def lslice(w):
        K, M = w.shape
        r = w.reshape(K // 128, 128, M // 128, 128)
        return np.ascontiguousarray(r.transpose(2, 1, 0, 3)).reshape(
            M // 128, 128, (K // 128) * 128).astype(np.float32)

    out["wq_l"] = lslice(W["wq"])
    out["wk_l"] = lslice(W["wk"])
    out["wv_l"] = lslice(W["wv"])
    wo = W["wo"]
    r = wo.reshape(DC, 128, OB, 512)
    out["wo_r"] = np.ascontiguousarray(r.transpose(2, 0, 1, 3)).astype(np.float32)
    Wf = W["W"]
    r = Wf.reshape(OB, 512, DC, 128)
    out["W_r"] = np.ascontiguousarray(r.transpose(0, 2, 3, 1)).astype(np.float32)
    A = W["A"]
    r = A.reshape(MAXR, DC, 128)
    out["A_l"] = np.ascontiguousarray(r.transpose(2, 1, 0)).reshape(128, DC * MAXR).astype(np.float32)
    out["B_l"] = np.ascontiguousarray(W["B"].T).astype(np.float32)
    g = W["gate_w"].reshape(NG, DC, 128)
    out["g_l"] = np.ascontiguousarray(g.transpose(2, 1, 0)).reshape(128, DC * NG).astype(np.float32)
    out["gb_c"] = np.ascontiguousarray(W["gate_b"].reshape(NG, 1)).astype(np.float32)
    out["b_row"] = np.ascontiguousarray(W["b"].reshape(1, D)).astype(np.float32)
    return out


def make_in_maps(cfg, x, wdict):
    c = derived(cfg)
    consts = host_constants(cfg)
    wmats = host_weights(cfg, wdict)
    n_cores = c["B"] * c["NCORE_PER_B"]
    in_maps = []
    for core in range(n_cores):
        m = dict(consts)
        m.update(wmats)
        m.update(host_percore(cfg, core, x))
        in_maps.append(m)
    return in_maps


def kernel(L0_attn_x, L0_attn_wq, L0_attn_wk, L0_attn_wv, L0_attn_wo,
           L0_ffn_W, L0_ffn_b, L0_ffn_A, L0_ffn_B, L0_ffn_gate_w, L0_ffn_gate_b):
    cfg = REAL
    c = derived(cfg)
    x = np.asarray(L0_attn_x, dtype=np.float32)
    in_maps = make_in_maps(cfg, x, dict(
        wq=np.asarray(L0_attn_wq, np.float32), wk=np.asarray(L0_attn_wk, np.float32),
        wv=np.asarray(L0_attn_wv, np.float32), wo=np.asarray(L0_attn_wo, np.float32),
        W=np.asarray(L0_ffn_W, np.float32), b=np.asarray(L0_ffn_b, np.float32),
        A=np.asarray(L0_ffn_A, np.float32), B=np.asarray(L0_ffn_B, np.float32),
        gate_w=np.asarray(L0_ffn_gate_w, np.float32),
        gate_b=np.asarray(L0_ffn_gate_b, np.float32)))
    nc = build_program(cfg)
    res = run_bass_kernel_spmd(nc, in_maps, core_ids=list(range(8)))
    B, S, Dm = x.shape
    out = np.empty((B, S, Dm), dtype=np.float32)
    for core in range(8):
        cb, ci = core // c["NCORE_PER_B"], core % c["NCORE_PER_B"]
        out[cb, ci * c["OWN"]:(ci + 1) * c["OWN"], :] = res.results[core]["y"]
    return out


# version 30
# speedup vs baseline: 1.0285x; 1.0206x over previous
import contextlib
import numpy as np
import concourse.bass as bass
import concourse.mybir as mybir
from concourse.bass_utils import run_bass_kernel_spmd

F32 = mybir.dt.float32
BF16 = mybir.dt.bfloat16
U32 = mybir.dt.uint32
AF = mybir.ActivationFunctionType
ALU = mybir.AluOpType

REAL = dict(
    B=2, S=4096, D=4096, HQ=32, HKV=8, HD=128, WIN=1024,
    OWN=1024, RANKS=(8, 16, 32, 64), EPS=1e-5, THETA=10000.0,
)


def derived(cfg):
    c = dict(cfg)
    c["G"] = c["HQ"] // c["HKV"]
    c["DC"] = c["D"] // 128
    c["EXT"] = c["OWN"] + c["WIN"]
    c["KT"] = c["EXT"] // 128
    c["NPASS"] = 2
    c["PT"] = c["OWN"] // c["NPASS"]
    c["QS"] = c["PT"] // 128
    c["NSUB"] = c["OWN"] // 128
    c["KW"] = c["WIN"] + 128
    c["OB"] = c["D"] // 512
    c["MAXR"] = max(c["RANKS"])
    c["NG"] = len(c["RANKS"])
    c["SM"] = c["HD"] ** -0.5
    c["NCORE_PER_B"] = c["S"] // c["OWN"]
    c["GRP"] = cfg.get("GRP_OVR") or min(512, c["EXT"])
    c["HGRP"] = c["EXT"] // c["GRP"]
    return c


class Res:
    __slots__ = ("w", "r", "dmasem")

    def __init__(self):
        self.w = None
        self.r = {}
        self.dmasem = None

    def take(self, other):
        toks = list(other.r.items())
        if other.w is not None:
            toks.append(other.w)
        for q, v in toks:
            if self.r.get(q, 0) < v:
                self.r[q] = v


class Prog:
    QUEUES = ("pe", "act", "dve", "gdma", "sdma", "pool")
    DMAQ = ("gdma", "sdma")

    @staticmethod
    def stream(q):
        return "gdma" if q == "pool" else q

    def __init__(self, nc, es):
        self.nc = nc
        self.es = es
        self.semh = {}
        self.cnt = {}
        for q in self.QUEUES:
            if q not in self.DMAQ:
                self.semh[q] = es.enter_context(nc.semaphore("s_" + q))
                self.cnt[q] = 0
        self.known = {self.stream(q): {} for q in self.QUEUES}
        self.items = {self.stream(q): [] for q in self.QUEUES}
        self._nds = 0

    def new_dmasem(self):
        name = "ds%d" % self._nds
        self._nds += 1
        self.semh[name] = self.es.enter_context(self.nc.semaphore(name))
        self.cnt[name] = 0
        return name

    def _wait(self, q, tok):
        if tok is None:
            return
        q = self.stream(q)
        pq, v = tok
        if v <= 0 or self.known[q].get(pq, 0) >= v:
            return
        sem = self.semh[pq]
        self.items[q].append(lambda e, sem=sem, v=v: e.wait_ge(sem, v))
        self.known[q][pq] = v

    def do(self, q, fns, reads=(), writes=(), extra_waits=()):
        if not isinstance(fns, (list, tuple)):
            fns = [fns]
        is_dma = q in self.DMAQ
        if is_dma:
            res0 = writes[0]
            if res0.dmasem is None:
                res0.dmasem = self.new_dmasem()
            sname = res0.dmasem
            step = 16
        else:
            sname = q
            step = 1
        for t in extra_waits:
            self._wait(q, t)
        for res in reads:
            self._wait(q, res.w)
        for res in writes:
            if res.w is not None and not (is_dma and res.w[0] == sname):
                self._wait(q, res.w)
            for pq, v in res.r.items():
                self._wait(q, (pq, v))
        sem = self.semh[sname]
        sq = self.stream(q)
        for fn in fns[:-1]:
            self.items[sq].append(lambda e, fn=fn: fn(e))
        last = fns[-1]

        def wrapped(e, fn=last, sem=sem, step=step):
            fn(e).then_inc(sem, step)
        self.items[sq].append(wrapped)
        self.cnt[sname] += step
        tok = (sname, self.cnt[sname])
        for res in reads:
            if res.r.get(q, 0) < self.cnt.get(q, 0) and not is_dma:
                pass
        if not is_dma:
            for res in reads:
                if res.r.get(sname, 0) < tok[1]:
                    res.r[sname] = tok[1]
        else:
            for res in reads:
                if res.r.get(sname, 0) < tok[1]:
                    res.r[sname] = tok[1]
        for res in writes:
            res.w = tok
            res.r = {}
        return tok

    def raw(self, q, fn):
        self.items[q].append(fn)


class Arena:
    def __init__(self, t):
        self.t = t

    def bf(self, off, n, p0=0, p1=128):
        return self.t[p0:p1, off:off + n]

    def f32(self, off, n, p0=0, p1=128):
        assert off % 2 == 0
        return self.t[:, :].bitcast(F32)[p0:p1, off // 2: off // 2 + n]


def build_program(cfg):
    c = derived(cfg)
    D, DC, HQ, HKV, G = c["D"], c["DC"], c["HQ"], c["HKV"], c["G"]
    OWN, EXT, WIN, KW = c["OWN"], c["EXT"], c["WIN"], c["KW"]
    NPASS, PT, QS, KT, OB = c["NPASS"], c["PT"], c["QS"], c["KT"], c["OB"]
    NSUB, MAXR, NG = c["NSUB"], c["MAXR"], c["NG"]
    GRP, HGRP = c["GRP"], c["HGRP"]
    HC = DC // 2

    nc = bass.Bass("TRN2", target_bir_lowering=False)

    def din(name, shape):
        return nc.dram_tensor(name, list(shape), F32, kind="ExternalInput").ap()

    xo = din("xo", (OWN, D))
    xh = din("xh", (WIN, D))
    wq_l = din("wq_l", (HQ, 128, DC * 128))
    wk_l = din("wk_l", (HKV, 128, DC * 128))
    wv_l = din("wv_l", (HKV, 128, DC * 128))
    wo_r = din("wo_r", (OB, HQ, 128, 512))
    W_r = din("W_r", (OB, DC, 128, 512))
    A_l = din("A_l", (128, DC * MAXR))
    B_l = din("B_l", (MAXR, D))
    g_l = din("g_l", (128, DC * NG))
    gb_c = din("gb_c", (NG, 1))
    b_row = din("b_row", (1, D))
    cos_e = din("cos_e", (128, EXT))
    sin_e = din("sin_e", (128, EXT))
    corr_l = din("corr_l", (128, NSUB))
    idnt = din("idnt", (128, 128))
    p128 = din("p128", (128, 128))
    m1t = din("m1t", (128, 128))
    m2t = din("m2t", (128, 128))
    thr = din("thr", (MAXR, 1))
    ones_r = din("ones_r", (1, 512))
    y = nc.dram_tensor("y", [OWN, D], F32, kind="ExternalOutput").ap()

    with contextlib.ExitStack() as es:
        E = es.enter_context
        cols_xg = DC * GRP
        cols_D = HQ * PT
        cols_xfm = DC * PT
        cols_krot = HKV * EXT
        cols_vtm = KT * HKV * 128
        cols_z1 = DC * PT
        B1 = E(nc.sbuf_tensor("B1", [128, max(cols_xg, cols_D, 4 * D)], BF16))
        B2 = E(nc.sbuf_tensor("B2", [128, max(cols_xg, cols_xfm, 4 * D,
                                              DC * MAXR + 3 * D + OWN)], BF16))
        B3 = E(nc.sbuf_tensor("B3", [128, max(cols_krot, cols_z1, 4 * D)], BF16))
        B4 = E(nc.sbuf_tensor("B4", [128, max(cols_vtm, 4 * D)], BF16))
        B5 = E(nc.sbuf_tensor("B5", [128, max(cols_z1, 4 * D, cols_D)], BF16))
        a1, a2, a3, a4, a5 = Arena(B1), Arena(B2), Arena(B3), Arena(B4), Arena(B5)

        c_cos = E(nc.sbuf_tensor("c_cos", [128, EXT], BF16))
        c_sin = E(nc.sbuf_tensor("c_sin", [128, EXT], BF16))
        c_id = E(nc.sbuf_tensor("c_id", [128, 128], BF16))
        c_idf = E(nc.sbuf_tensor("c_idf", [128, 128], F32))
        c_p128 = E(nc.sbuf_tensor("c_p128", [128, 128], BF16))
        c_m1 = E(nc.sbuf_tensor("c_m1", [128, 128], BF16))
        c_m2 = E(nc.sbuf_tensor("c_m2", [128, 128], BF16))
        c_ones = E(nc.sbuf_tensor("c_ones", [1, 512], BF16))
        c_thr = E(nc.sbuf_tensor("c_thr", [MAXR, 1], F32))
        c_corr = E(nc.sbuf_tensor("c_corr", [128, NSUB], F32))
        c_gbc = E(nc.sbuf_tensor("c_gbc", [NG, 1], F32))
        c_g = E(nc.sbuf_tensor("c_g", [128, DC * NG], BF16))

        SCR_N = 14720
        SCR = E(nc.sbuf_tensor("SCR", [128, SCR_N], BF16))
        scr = Arena(SCR)
        SLC_N = DC * 128
        SLC = (0, SLC_N)
        WRS = (2 * SLC_N, 2 * SLC_N + 512)
        base = 2 * SLC_N + 1024
        KRAW = base
        VRAW = KRAW + GRP
        SC1K = VRAW + GRP
        SC2K = SC1K + GRP
        assert SC2K + GRP <= SCR_N, SC2K + GRP
        QRAW = base
        QROT = QRAW + PT
        SC1 = QROT + PT
        SC2 = SC1 + PT
        PB = (SC2 + PT, SC2 + PT + KW)
        PTS = PB[1] + KW
        assert PTS + KW <= SCR_N, PTS + KW
        MSK = base
        Z1T = MSK + 512
        G8 = Z1T + 512
        GSB = G8 + 64
        IDXROW = GSB + 1024
        assert IDXROW + 512 <= SCR_N

        sums3 = E(nc.sbuf_tensor("sums3", [128, 3], F32))
        sumt = E(nc.sbuf_tensor("sumt", [128, 1], F32))
        rcp = E(nc.sbuf_tensor("rcp", [128, 1], F32))
        NSLOT = 4
        stats = E(nc.sbuf_tensor("stats", [128, 6 * (D // 512) * NSLOT], F32))
        mv = E(nc.sbuf_tensor("mv", [128, 2 * NSLOT], F32))
        rstd = E(nc.sbuf_tensor("rstd", [128, NSLOT], F32))
        nbias = E(nc.sbuf_tensor("nbias", [128, NSLOT], F32))
        gsb = scr.f32(GSB, 512, p0=0, p1=NG)
        mx8 = E(nc.sbuf_tensor("mx8", [128, 8], F32))
        idx8 = E(nc.sbuf_tensor("idx8", [128, 8], U32))
        idxf = E(nc.sbuf_tensor("idxf", [128, 1], F32))
        idxrow = scr.bf(IDXROW, PT, p0=0, p1=1)
        umsk = B2[0:MAXR, DC * MAXR + 3 * D: DC * MAXR + 3 * D + OWN]

        PS = E(nc.psum_tensor("PS", [128, 4096], F32))
        PSb = PS[:, :].bitcast(BF16)

        def bank(i):
            return PS[:, i * 512:(i + 1) * 512]

        def bankb(i):
            return PSb[:, i * 1024:(i + 1) * 1024]

        P = Prog(nc, es)

        r_const = Res()
        r_constS = Res()
        r_bank = [Res() for _ in range(8)]
        r_slc = [Res(), Res()]
        r_wrs = [Res(), Res()]
        r_xtm = [Res(), Res()]
        r_xg = [Res(), Res()]
        r_krot = Res()
        r_vtm = Res()
        r_kraw = Res()
        r_vraw = Res()
        r_sc12 = Res()
        r_xfm = Res()
        r_qraw = [Res(), Res()]
        r_qrot = [Res(), Res()]
        r_p = [Res(), Res()]
        r_pts = Res()
        r_D = Res()
        r_xres = Res()
        r_z1 = [Res(), Res()]
        r_z1tm = [Res(), Res()]
        r_sums = Res()
        r_sumt = Res()
        r_rcp = Res()
        r_stats = [Res() for _ in range(4)]
        r_mv = [Res() for _ in range(4)]
        r_rstd = [Res() for _ in range(4)]
        r_nbias = [Res() for _ in range(4)]
        r_gate = Res()
        r_idxrow = Res()
        r_umsk = Res()
        r_msk = Res()
        r_z2out = Res()
        r_y = [Res(), Res()]

        def ld_cast(dst, src):
            return lambda e: e.dma_start(out=dst, in_=src, max_dma_last_dim=8192)

        def ld_plain(dst, src):
            return lambda e: e.dma_start(out=dst, in_=src)

        for dst, src in [(c_id[:, :], idnt), (c_p128[:, :], p128), (c_m1[:, :], m1t),
                         (c_m2[:, :], m2t), (c_ones[:, :], ones_r),
                         (c_g[:, :], g_l), (c_cos[:, :], cos_e), (c_sin[:, :], sin_e)]:
            P.do("gdma", ld_cast(dst, src), writes=[r_const])
        for dst, src in [(c_idf[:, :], idnt), (c_thr[:, :], thr),
                         (c_corr[:, :], corr_l), (c_gbc[:, :], gb_c)]:
            P.do("sdma", ld_plain(dst, src), writes=[r_constS])
        RC = [r_const, r_constS]

        def xg_ap(g, kc, col0, ncol):
            buf = B2 if g % 2 == 0 else B1
            return buf[:, kc * GRP + col0: kc * GRP + col0 + ncol]

        def transpose_batches():
            out = []
            s = 0
            while s < DC:
                e_ = min(s + 8, DC)
                out.append(list(range(s, e_)))
                s = e_
            return out

        ntile_g = GRP // 128
        kv_bk = [0]

        def tile_stage(g, tt):
            ext_t = g * GRP + tt * 128
            if ext_t < WIN:
                tsrc = xh[ext_t:ext_t + 128, :]
            else:
                tsrc = xo[ext_t - WIN:ext_t - WIN + 128, :]
            sl = tt % 2
            xtm = a5.bf(sl * D, D)
            P.do("gdma", ld_cast(xtm, tsrc), writes=[r_xtm[sl]])
            xgbuf = B2 if g % 2 == 0 else B1
            for chunks in transpose_batches():
                bk = 5 + kv_bk[0] % 3
                kv_bk[0] += 1
                fns = []
                for j, dc in enumerate(chunks):
                    fns.append(lambda e, dc=dc, bk=bk, j=j, xtm=xtm: e.transpose(
                        bankb(bk)[:, j * 128:(j + 1) * 128],
                        xtm[:, dc * 128:(dc + 1) * 128], c_id[:, :]))
                P.do("pe", fns, reads=[r_xtm[sl]] + RC, writes=[r_bank[bk]])
                dc0 = chunks[0]
                dap = xgbuf[:, :].rearrange("p (c t) -> p c t", t=GRP)[
                    :, dc0:dc0 + len(chunks), tt * 128:(tt + 1) * 128]
                P.do("dve", lambda e, bk=bk, dap=dap, n=len(chunks): e.tensor_copy(
                    dap, bankb(bk)[:, 0:n * 128].rearrange("p (c t) -> p c t", t=128)),
                    reads=[r_bank[bk]], writes=[r_xg[g % 2]])

        for tt in range(ntile_g):
            tile_stage(0, tt)
        for g in range(HGRP):
            rxg = r_xg[g % 2]
            next_tiles = list(range(ntile_g)) if g + 1 < HGRP else []
            kv_tail = []
            for oc in range(2 * HKV + 1):
                if oc < 2 * HKV:
                    is_k = oc < HKV
                    hk = oc if is_k else oc - HKV
                    sl = oc % 2
                    pjb = oc % 2
                    wsl = scr.bf(SLC[sl], SLC_N)
                    P.do("gdma", ld_cast(wsl, (wk_l if is_k else wv_l)[hk]),
                         writes=[r_slc[sl]])
                    fns = []
                    for kc in range(DC):
                        fns.append(lambda e, kc=kc, wsl=wsl, pjb=pjb, g=g: e.matmul(
                            PS[:, pjb * 512: pjb * 512 + GRP], wsl[:, kc * 128:(kc + 1) * 128],
                            xg_ap(g, kc, 0, GRP),
                            start=(kc == 0), stop=(kc == DC - 1)))
                    P.do("pe", fns, reads=[r_slc[sl], rxg] + RC, writes=[r_bank[pjb]])
                step_t = max(1, (2 * HKV - 1) // ntile_g)
                if next_tiles and oc >= 2 and (oc - 2) % step_t == 0:
                    tile_stage(g + 1, next_tiles.pop(0))
                if next_tiles and oc == 2 * HKV:
                    while next_tiles:
                        tile_stage(g + 1, next_tiles.pop(0))
                if kv_tail:
                    kv_tail.pop(0)()
                if oc >= 2 * HKV:
                    continue

                def kv_stage(is_k=is_k, hk=hk, pjb=pjb, g=g):
                    if is_k:
                        swb = 2 + hk % 2
                        kraw = scr.bf(KRAW, GRP)
                        P.do("act", lambda e, kraw=kraw, pjb=pjb: e.activation(
                            kraw, PS[:, pjb * 512: pjb * 512 + GRP], AF.Copy),
                            reads=[r_bank[pjb]], writes=[r_kraw])
                        P.do("pe", lambda e, kraw=kraw, swb=swb: e.matmul(
                            PS[:, swb * 512: swb * 512 + GRP], c_p128[:, :], kraw,
                            start=True, stop=True),
                            reads=[r_kraw] + RC, writes=[r_bank[swb]])
                        s1 = scr.bf(SC1K, GRP)
                        s2 = scr.bf(SC2K, GRP)
                        t_a = P.do("dve", lambda e, s1=s1, g=g, swb=swb: e.tensor_mul(
                            s1, PS[:, swb * 512: swb * 512 + GRP], c_sin[:, g * GRP:(g + 1) * GRP]),
                            reads=[r_bank[swb]] + RC, writes=[r_sc12])
                        t_b = P.do("dve", lambda e, s2=s2, kraw=kraw, g=g: e.tensor_mul(
                            s2, kraw, c_cos[:, g * GRP:(g + 1) * GRP]),
                            reads=[r_kraw] + RC, writes=[r_sc12])
                        P.do("dve", lambda e, s1=s1, s2=s2, hk=hk, g=g: e.tensor_add(
                            B3[:, hk * EXT + g * GRP: hk * EXT + (g + 1) * GRP], s1, s2),
                            reads=[r_sc12], writes=[r_krot], extra_waits=[t_a, t_b])
                    else:
                        vraw = scr.bf(VRAW, GRP)
                        P.do("act", lambda e, vraw=vraw, pjb=pjb: e.activation(
                            vraw, PS[:, pjb * 512: pjb * 512 + GRP], AF.Copy),
                            reads=[r_bank[pjb]], writes=[r_vraw])
                        fns = []
                        for tt in range(ntile_g):
                            fns.append(lambda e, tt=tt, vraw=vraw: e.transpose(
                                bankb(4)[:, tt * 128:(tt + 1) * 128],
                                vraw[:, tt * 128:(tt + 1) * 128], c_id[:, :]))
                        P.do("pe", fns, reads=[r_vraw] + RC, writes=[r_bank[4]])
                        vdst = B4[:, :].rearrange("p (kt h d) -> p kt h d", h=HKV, d=128)[
                            :, g * ntile_g:(g + 1) * ntile_g, hk, :]
                        P.do("dve", lambda e, vdst=vdst: e.tensor_copy(
                            vdst, bankb(4)[:, 0:ntile_g * 128].rearrange("p (t d) -> p t d", d=128)),
                            reads=[r_bank[4]], writes=[r_vtm])
                kv_tail.append(kv_stage)

        NST6 = 6 * (D // 512)

        def ln_chain(slot, zpre, rzpre, out, rout, extra_reads=(), skip_stats=False):
            st = stats[:, slot * NST6:(slot + 1) * NST6]
            mvs = mv[:, slot * 2:(slot + 1) * 2]
            rs = rstd[:, slot:slot + 1]
            nb = nbias[:, slot:slot + 1]
            nst = D // 512
            if not skip_stats:
                fns = []
                for q5 in range(nst):
                    fns.append(lambda e, q5=q5, zpre=zpre, st=st: e.bn_stats(
                        st[:, q5 * 6:(q5 + 1) * 6], zpre[:, q5 * 512:(q5 + 1) * 512]))
                P.do("dve", fns, reads=[rzpre], writes=[r_stats[slot]])
            P.do("dve", lambda e, st=st, mvs=mvs: e.bn_aggr(mvs, st),
                 reads=[r_stats[slot]], writes=[r_mv[slot]])
            P.do("dve", lambda e, mvs=mvs, rs=rs: e.tensor_scalar_add(
                rs, mvs[:, 1:2], float(c["EPS"])),
                reads=[r_mv[slot]], writes=[r_rstd[slot]])
            P.do("dve", lambda e, rs=rs: e.reciprocal(rs, rs),
                 reads=[r_rstd[slot]], writes=[r_rstd[slot]])
            P.do("act", lambda e, rs=rs: e.activation(rs, rs, AF.Sqrt),
                 reads=[r_rstd[slot]], writes=[r_rstd[slot]])
            P.do("dve", lambda e, mvs=mvs, rs=rs, nb=nb: e.tensor_mul(nb, mvs[:, 0:1], rs),
                 reads=[r_mv[slot], r_rstd[slot]], writes=[r_nbias[slot]])
            P.do("dve", lambda e, nb=nb: e.tensor_scalar_mul(nb, nb, -1.0),
                 reads=[r_nbias[slot]], writes=[r_nbias[slot]])
            P.do("act", lambda e, out=out, zpre=zpre, rs=rs, nb=nb: e.activation(
                out, zpre, AF.Identity, bias=nb, scale=rs),
                reads=[rzpre, r_rstd[slot], r_nbias[slot]] + list(extra_reads),
                writes=[rout])

        sm = c["SM"]
        vview = B4[:, :].rearrange("p (kt h d) -> p kt h d", h=HKV, d=128)
        for r in (r_qraw[0], r_qraw[1], r_qrot[0], r_qrot[1], r_p[0], r_p[1],
                  r_pts, r_wrs[0], r_wrs[1]):
            r.take(r_kraw)
            r.take(r_vraw)
            r.take(r_sc12)
        NOREBUILD = (PT == GRP and WIN % GRP == 0 and not cfg.get("FORCE_REBUILD"))
        for p_i in range(NPASS):
            tok0 = p_i * PT
            g_pass = (WIN + tok0) // GRP
            if NOREBUILD:
                xgb = B2 if g_pass % 2 == 0 else B1
                rxfm_r = r_xg[g_pass % 2]

                def xfm_ap(kc, xgb=xgb):
                    return xgb[:, kc * GRP: kc * GRP + PT]
                XR = Arena(xgb)
                if p_i == 0:
                    DBt = B5
                    r_D.take(r_xtm[0])
                    r_D.take(r_xtm[1])
                else:
                    g_prev = (WIN + (p_i - 1) * PT) // GRP
                    DBt = B2 if g_prev % 2 == 0 else B1
                    r_D.take(r_xres)
                    r_D.take(r_xg[g_prev % 2])
            else:
                rxfm_r = r_xfm

                def xfm_ap(kc):
                    return B2[:, kc * PT: (kc + 1) * PT]
                XR = a2
                DBt = B1
                r_xtm[0].take(r_D)
                r_xtm[1].take(r_D)
                for rr in (r_xtm[0], r_xtm[1]):
                    rr.take(r_xg[0])
                    rr.take(r_xg[1])
                    rr.take(r_z1tm[0])
                    rr.take(r_z1tm[1])
                r_xfm.take(r_xres)
                r_xfm.take(r_xg[0])
                r_xfm.take(r_xg[1])
                xa = a5 if p_i == 0 else a1
                for ts in range(QS):
                    sl = ts % 2
                    xtm = xa.bf(sl * D, D)
                    P.do("gdma", ld_cast(xtm, xo[tok0 + ts * 128: tok0 + (ts + 1) * 128, :]),
                         writes=[r_xtm[sl]])
                    for bi, chunks in enumerate(transpose_batches()):
                        bk = (ts % 2) * 4 + bi % 4
                        fns = []
                        for j, dc in enumerate(chunks):
                            fns.append(lambda e, dc=dc, bk=bk, j=j, xtm=xtm: e.transpose(
                                bankb(bk)[:, j * 128:(j + 1) * 128],
                                xtm[:, dc * 128:(dc + 1) * 128], c_id[:, :]))
                        P.do("pe", fns, reads=[r_xtm[sl]] + RC, writes=[r_bank[bk]])
                        dc0 = chunks[0]
                        dap = B2[:, :].rearrange("p (c t) -> p c t", t=PT)[
                            :, dc0:dc0 + len(chunks), ts * 128:(ts + 1) * 128]
                        P.do("dve", lambda e, bk=bk, dap=dap, n=len(chunks): e.tensor_copy(
                            dap, bankb(bk)[:, 0:n * 128].rearrange("p (c t) -> p c t", t=128)),
                            reads=[r_bank[bk]], writes=[r_xfm])
                r_D.take(r_xtm[0])
                r_D.take(r_xtm[1])
            tailsT = []
            tailsA = []
            qsc = [0]

            QR = (QRAW, 2 * SLC_N)
            QT = (QROT, 2 * SLC_N + 512)

            def emit_A_load(h):
                sl = h % 2
                wsl = scr.bf(SLC[sl], SLC_N)
                P.do("gdma", ld_cast(wsl, wq_l[h]), writes=[r_slc[sl]])

            def emit_A_piece(h, j, npiece):
                sl = h % 2
                wsl = scr.bf(SLC[sl], SLC_N)
                per = (DC + npiece - 1) // npiece
                lo, hi = j * per, min(DC, (j + 1) * per)
                fns = []
                for kc in range(lo, hi):
                    fns.append(lambda e, kc=kc, wsl=wsl, xfm_ap=xfm_ap: e.matmul(
                        PS[:, 7 * 512: 7 * 512 + PT], wsl[:, kc * 128:(kc + 1) * 128],
                        xfm_ap(kc),
                        start=(kc == 0), stop=(kc == DC - 1)))
                P.do("pe", fns, reads=[r_slc[sl], rxfm_r],
                     writes=[r_bank[7]] if j in (0, npiece - 1) else [])

            def emit_B1(h):
                hp = h % 2
                qraw = scr.bf(QR[hp], PT)
                P.do("act", lambda e, qraw=qraw: e.activation(
                    qraw, PS[:, 7 * 512: 7 * 512 + PT], AF.Copy, scale=float(sm)),
                    reads=[r_bank[7]], writes=[r_qraw[hp]])

            def emit_B2(h):
                hp = h % 2
                qraw = scr.bf(QR[hp], PT)
                qrot = scr.bf(QT[hp], PT)
                P.do("pe", lambda e, qraw=qraw: e.matmul(
                    PS[:, 7 * 512: 7 * 512 + PT], c_p128[:, :], qraw,
                    start=True, stop=True),
                    reads=[r_qraw[hp]] + RC, writes=[r_bank[7]])
                s1 = scr.bf(SC1, PT)
                s2 = scr.bf(SC2, PT)
                cs0 = WIN + tok0
                t_a = P.do("dve", lambda e, s1=s1, cs0=cs0: e.tensor_mul(
                    s1, PS[:, 7 * 512: 7 * 512 + PT], c_sin[:, cs0:cs0 + PT]),
                    reads=[r_bank[7]] + RC, writes=[r_sc12])
                t_b = P.do("dve", lambda e, s2=s2, qraw=qraw, cs0=cs0: e.tensor_mul(
                    s2, qraw, c_cos[:, cs0:cs0 + PT]),
                    reads=[r_qraw[hp]] + RC, writes=[r_sc12])
                P.do("dve", lambda e, qrot=qrot, s1=s1, s2=s2: e.tensor_add(qrot, s1, s2),
                     reads=[r_sc12], writes=[r_qrot[hp]], extra_waits=[t_a, t_b])

            def emit_C(h, s4):
                hk = h // G
                hp = h % 2
                qrot = scr.bf(QT[hp], PT)
                sg = p_i * QS + s4
                elo = sg * 128
                par = qsc[0] % 2
                qsc[0] += 1
                bs = 0 if par == 0 else 3
                pb = scr.bf(PB[par], KW)
                rp = r_p[par]
                schunks = []
                off = 0
                while off < KW:
                    wdt = min(512, KW - off)
                    schunks.append((off, wdt))
                    off += wdt
                nch = len(schunks)
                av_bk = (3 - bs) + nch - 1
                t9_bk = bs + nch - 1
                av_off = schunks[-1][1]
                n_extra = max(0, KW // 128 - 8)
                assert 2 * (av_off + 128) + n_extra * 128 <= 1024
                sbanks = [r_bank[bs + ci] for ci in range(nch)]
                for ci, (off, wdt) in enumerate(schunks):
                    bkk = bs + ci
                    has_m1 = off == 0
                    has_m2 = off + wdt == KW
                    fns = [lambda e, off=off, wdt=wdt, bkk=bkk, s4=s4, hk=hk, elo=elo, qrot=qrot, st=(not has_m1 and not has_m2):
                           e.matmul(PS[:, bkk * 512: bkk * 512 + wdt],
                                    qrot[:, s4 * 128:(s4 + 1) * 128],
                                    B3[:, hk * EXT + elo + off: hk * EXT + elo + off + wdt],
                                    start=True, stop=st)]
                    if has_m1:
                        fns.append(lambda e, bkk=bkk, st=(not has_m2): e.matmul(
                            PS[:, bkk * 512: bkk * 512 + 128], c_id[:, :], c_m1[:, :],
                            start=False, stop=st))
                    if has_m2:
                        fns.append(lambda e, bkk=bkk, off=off, wdt=wdt: e.matmul(
                            PS[:, bkk * 512 + wdt - 128: bkk * 512 + wdt],
                            c_id[:, :], c_m2[:, :], start=False, stop=True))
                    P.do("pe", fns, reads=[r_qrot[hp], r_krot] + RC, writes=[r_bank[bkk]])

                def emit_post(bs=bs, pb=pb, rp=rp, sg=sg, sbanks=sbanks):
                    P.do("act", lambda e, bs=bs, pb=pb: e.activation(
                        pb[:, 0:KW], PS[:, bs * 512: bs * 512 + KW],
                        AF.Exp, accum_out=sums3[:, 0:1]),
                        reads=sbanks, writes=[rp, r_sums])
                    P.do("dve", lambda e, sg=sg: e.tensor_add(
                        sumt[:, :], sums3[:, 0:1], c_corr[:, sg:sg + 1]),
                        reads=[r_sums] + RC, writes=[r_sumt])
                    P.do("dve", lambda e: e.reciprocal(rcp[:, :], sumt[:, :]),
                         reads=[r_sumt], writes=[r_rcp])
                    P.do("dve", lambda e, pb=pb: e.tensor_scalar_mul(pb, pb, rcp[:, :]),
                         reads=[rp, r_rcp], writes=[rp])

                def tailT(h=h, s4=s4, sg=sg, hk=hk, pb=pb, rp=rp,
                          av_bk=t9_bk, av_off=av_off):
                    nfull = KW // 128
                    pts = scr.bf(PTS, KW)
                    n6 = min(8, nfull)
                    fns = []
                    for j in range(n6):
                        fns.append(lambda e, j=j, pb=pb: e.transpose(
                            bankb(6)[:, j * 128:(j + 1) * 128],
                            pb[:, j * 128:(j + 1) * 128], c_id[:, :]))
                    extra = []
                    for j in range(n6, nfull):
                        xoff = 2 * (av_off + 128) + (j - n6) * 128
                        extra.append(xoff)
                        fns.append(lambda e, j=j, pb=pb, xoff=xoff, av_bk=av_bk: e.transpose(
                            bankb(av_bk)[:, xoff:xoff + 128],
                            pb[:, j * 128:(j + 1) * 128], c_id[:, :]))
                    P.do("pe", fns, reads=[rp] + RC, writes=[r_bank[6], r_bank[av_bk]])
                    P.do("dve", lambda e, pts=pts, n6=n6: e.tensor_copy(
                        pts[:, 0:n6 * 128], bankb(6)[:, 0:n6 * 128]),
                        reads=[r_bank[6]], writes=[r_pts])
                    if nfull > n6:
                        P.do("dve", lambda e, pts=pts, n6=n6, nfull=nfull, av_bk=av_bk,
                             x0=extra[0]: e.tensor_copy(
                            pts[:, n6 * 128:nfull * 128],
                            bankb(av_bk)[:, x0:x0 + (nfull - n6) * 128]),
                            reads=[r_bank[av_bk]], writes=[r_pts])

                def tailA(h=h, s4=s4, sg=sg, hk=hk,
                          av_bk=av_bk, av_off=av_off):
                    nfull = KW // 128
                    pts = scr.bf(PTS, KW)
                    fns = []
                    for j in range(nfull):
                        kt = sg + j
                        fns.append(lambda e, j=j, kt=kt, hk=hk, pts=pts:
                                   e.matmul(PS[:, av_bk * 512 + av_off: av_bk * 512 + av_off + 128],
                                            vview[:, kt, hk, :],
                                            pts[:, j * 128:(j + 1) * 128],
                                            start=(j == 0), stop=(j == nfull - 1)))
                    P.do("pe", fns, reads=[r_pts, r_vtm], writes=[r_bank[av_bk]])
                    P.do("act", lambda e, DBt=DBt: e.activation(
                        DBt[:, h * PT + s4 * 128: h * PT + (s4 + 1) * 128],
                        PS[:, av_bk * 512 + av_off: av_bk * 512 + av_off + 128],
                        AF.Copy),
                        reads=[r_bank[av_bk]], writes=[r_D])
                return emit_post, tailT, tailA

            NPIECE = max(1, QS - 1)
            emit_A_load(0)
            for j in range(NPIECE):
                emit_A_piece(0, j, NPIECE)
            emit_B1(0)
            emit_B2(0)
            for h in range(HQ):
                if h + 1 < HQ:
                    emit_A_load(h + 1)
                for s4 in range(QS):
                    post, tT, tA = emit_C(h, s4)
                    post()
                    if len(tailsA) >= 2:
                        tailsA.pop(0)()
                    if s4 < NPIECE and h + 1 < HQ:
                        emit_A_piece(h + 1, s4, NPIECE)
                    if s4 == NPIECE - 1 and h + 1 < HQ:
                        emit_B1(h + 1)
                    if tailsT:
                        tailsT.pop(0)()
                    tailsT.append(tT)
                    tailsA.append(tA)
                    if s4 == NPIECE - 1 and h + 1 < HQ:
                        emit_B2(h + 1)
            while tailsT or tailsA:
                if tailsA:
                    tailsA.pop(0)()
                if tailsT:
                    tailsT.pop(0)()
            if cfg.get("DEBUG_D") and p_i == 0:
                P.do("gdma", lambda e, DBt=DBt: e.dma_start(
                    out=y[0:128, :].rearrange("p (a b) -> p a b", a=2) if False else y[0:256, :].rearrange("(a p) m -> p a m", p=128),
                    in_=DBt[:, 0:HQ * PT].rearrange("p (a m) -> p a m", a=2),
                    max_dma_last_dim=8192),
                    reads=[r_D], writes=[r_y[0]])
            r_xres.take(rxfm_r)
            for ts_ in range(QS):
                P.do("gdma", ld_cast(XR.bf(ts_ * D, D),
                                     xo[tok0 + ts_ * 128: tok0 + (ts_ + 1) * 128, :]),
                     writes=[r_xres])
            ch_sz = max(1, min(8, HQ, SLC_N // 512))
            n_ch = HQ // ch_sz
            for ob in range(OB):
                lastw = None
                for ch in range(n_ch):
                    wch = scr.bf(SLC[ch % 2], ch_sz * 512)
                    P.do("gdma", lambda e, wch=wch, ob=ob, ch=ch: e.dma_start(
                        out=wch[:, :].rearrange("p (s m) -> p s m", m=512),
                        in_=wo_r[ob, ch * ch_sz:(ch + 1) * ch_sz].rearrange("s p m -> p s m"),
                        max_dma_last_dim=8192),
                        writes=[r_slc[ch % 2]])
                    for kcl in range(ch_sz):
                        kc = ch * ch_sz + kcl
                        fns = []
                        for ts in range(QS):
                            fns.append(lambda e, ts=ts, kc=kc, wch=wch, kcl=kcl, DBt=DBt: e.matmul(
                                bank(ts), DBt[:, kc * PT + ts * 128: kc * PT + (ts + 1) * 128],
                                wch[:, kcl * 512:(kcl + 1) * 512],
                                start=(kc == 0), stop=(kc == HQ - 1)))
                        wset = [r_bank[t] for t in range(QS)] if kc in (0, HQ - 1) else []
                        lastw = P.do("pe", fns, reads=[r_slc[ch % 2], r_D], writes=wset)
                for ts in range(QS):
                    xres_sl = XR.bf(ts * D + ob * 512, 512)
                    P.do("dve", lambda e, ts=ts, xres_sl=xres_sl: e.tensor_add(
                        xres_sl, bank(ts), xres_sl),
                        reads=[r_bank[ts], r_xres], writes=[r_xres],
                        extra_waits=[lastw])
                    slot = ts % NSLOT
                    P.do("dve", lambda e, xres_sl=xres_sl, slot=slot, ob=ob: e.bn_stats(
                        stats[:, slot * NST6 + ob * 6: slot * NST6 + (ob + 1) * 6], xres_sl),
                        reads=[r_xres], writes=[r_stats[slot]])
            z1buf = B5 if p_i == 0 else B3
            rz1 = r_z1[p_i]
            if p_i == 0:
                rz1.take(r_xtm[0])
                rz1.take(r_xtm[1])
            else:
                rz1.take(r_krot)
            if NOREBUILD and p_i == 0:
                rz1.take(r_D)
            DBGPRE_ONLY = cfg.get("DEBUG_PRE") and not cfg.get("DEBUG_Z1")
            if not NOREBUILD:
                r_z1tm[0].take(r_D)
                r_z1tm[1].take(r_D)
            for ts in range(QS):
                zpre = XR.bf(ts * D, D)
                if NOREBUILD:
                    z1tm = zpre
                    rzt = r_xres
                else:
                    z1tm = a1.bf((ts % 2) * D, D)
                    rzt = r_z1tm[ts % 2]
                if cfg.get("DEBUG_PRE"):
                    P.do("gdma", lambda e, zpre=zpre, tok0=tok0, ts=ts: e.dma_start(
                        out=y[tok0 + ts * 128: tok0 + (ts + 1) * 128, :],
                        in_=zpre, max_dma_last_dim=8192),
                        reads=[r_xres], writes=[r_y[0]])
                ln_chain(ts % NSLOT, zpre, r_xres, z1tm, rzt, skip_stats=True)
                if cfg.get("DEBUG_Z1"):
                    P.do("gdma", lambda e, z1tm=z1tm, tok0=tok0, ts=ts: e.dma_start(
                        out=y[tok0 + ts * 128: tok0 + (ts + 1) * 128, :],
                        in_=z1tm, max_dma_last_dim=8192),
                        reads=[rzt], writes=[r_y[0]])
                for bi, chunks in enumerate(transpose_batches()):
                    bk = 4 + ((ts % 2) * 2 + bi) % 4
                    fns = []
                    for j, dc in enumerate(chunks):
                        fns.append(lambda e, dc=dc, bk=bk, j=j, z1tm=z1tm: e.transpose(
                            bankb(bk)[:, j * 128:(j + 1) * 128],
                            z1tm[:, dc * 128:(dc + 1) * 128], c_id[:, :]))
                    P.do("pe", fns, reads=[rzt] + RC, writes=[r_bank[bk]])
                    dc0 = chunks[0]
                    dap = z1buf[:, :].rearrange("p (c t) -> p c t", t=PT)[
                        :, dc0:dc0 + len(chunks), ts * 128:(ts + 1) * 128]
                    P.do("dve", lambda e, bk=bk, dap=dap, n=len(chunks): e.tensor_copy(
                        dap, bankb(bk)[:, 0:n * 128].rearrange("p (c t) -> p c t", t=128)),
                        reads=[r_bank[bk]], writes=[rz1])

        if cfg.get("DEBUG_Z1") or cfg.get("DEBUG_PRE"):
            for ry in r_y:
                if ry.dmasem is not None:
                    ysem = P.semh[ry.dmasem]
                    P.raw("sdma", lambda e, ysem=ysem, v=P.cnt[ry.dmasem]: e.wait_ge(ysem, v))
            return _finish(nc, P, es)
        cA = a2.bf(0, DC * MAXR)
        cB_off = DC * MAXR
        cB = a2.bf(cB_off, D, p0=0, p1=MAXR)
        r_cA = Res()
        r_cA.take(r_xres)
        r_cA.take(r_D)
        r_cB = Res()
        r_cB.take(r_xres)
        r_cB.take(r_D)
        P.do("gdma", ld_cast(cA, A_l), writes=[r_cA])
        P.do("gdma", ld_cast(cB, B_l), writes=[r_cB])

        def z1_ap(dc, col0, ncol):
            buf = B5 if col0 < PT else B3
            base = col0 if col0 < PT else col0 - PT
            return buf[:, dc * PT + base: dc * PT + base + ncol]

        c_brow = B2[0:1, cB_off + D: cB_off + 2 * D]
        r_brow = Res()
        r_brow.take(r_xres)
        r_brow.take(r_D)
        for rr in (r_msk, r_sc12):
            for src_r in (r_slc[0], r_slc[1], r_qraw[0], r_qraw[1], r_qrot[0],
                          r_qrot[1], r_p[0], r_p[1], r_pts, r_kraw, r_vraw):
                rr.take(src_r)
        r_msk.take(r_sc12)
        r_umsk.take(r_D)
        r_umsk.take(r_xres)
        P.do("gdma", ld_cast(c_brow, b_row), writes=[r_brow])
        for th in range(NPASS):
            rz = r_z1[th]
            fns = []
            for dc in range(DC):
                fns.append(lambda e, dc=dc, th=th: e.matmul(
                    PS[0:MAXR, 0:PT], cA[:, dc * MAXR:(dc + 1) * MAXR],
                    z1_ap(dc, th * PT, PT), start=(dc == 0), stop=(dc == DC - 1)))
            P.do("pe", fns, reads=[r_cA, rz], writes=[r_bank[0]])
            fns = []
            for dc in range(DC):
                fns.append(lambda e, dc=dc, th=th: e.matmul(
                    PS[0:NG, 512:512 + PT], c_g[:, dc * NG:(dc + 1) * NG],
                    z1_ap(dc, th * PT, PT), start=(dc == 0), stop=(dc == DC - 1)))
            P.do("pe", fns, reads=[rz] + RC, writes=[r_bank[1]])
            P.do("act", lambda e: e.activation(gsb[:, 0:PT], PS[0:NG, 512:512 + PT],
                                               AF.Identity, bias=c_gbc[:, :]),
                 reads=[r_bank[1]] + RC, writes=[r_gate])
            for ts in range(QS):
                P.do("pe", lambda e, ts=ts: e.transpose(
                    PS[0:128, 1024 + ts * 4: 1024 + (ts + 1) * 4],
                    gsb[:, ts * 128:(ts + 1) * 128], c_idf[0:NG, 0:NG]),
                    reads=[r_gate] + RC, writes=[r_bank[2]])
                g8 = scr.f32(G8, 8)
                P.do("dve", lambda e, g8=g8: e.memset(g8, -1e30), writes=[r_msk])
                P.do("dve", lambda e, ts=ts, g8=g8: e.tensor_copy(
                    g8[:, 0:NG], PS[0:128, 1024 + ts * 4:1024 + (ts + 1) * 4]),
                    reads=[r_bank[2], r_msk], writes=[r_msk])
                P.do("dve", lambda e, g8=g8: e.max(mx8[:, :], g8),
                     reads=[r_msk], writes=[r_gate])
                P.do("dve", lambda e, g8=g8: e.max_index(idx8[:, :], mx8[:, :], g8),
                     reads=[r_gate, r_msk], writes=[r_gate])
                P.do("dve", lambda e: e.tensor_copy(idxf[:, :], idx8[:, 0:1]),
                     reads=[r_gate], writes=[r_gate])
                P.do("pe", lambda e, ts=ts: e.transpose(
                    PS[0:1, 1536 + ts * 128: 1536 + (ts + 1) * 128], idxf[:, :], c_idf[:, :]),
                    reads=[r_gate] + RC, writes=[r_bank[3]])
                P.do("dve", lambda e, ts=ts: e.tensor_copy(
                    idxrow[:, ts * 128:(ts + 1) * 128],
                    PS[0:1, 1536 + ts * 128:1536 + (ts + 1) * 128]),
                    reads=[r_bank[3]], writes=[r_idxrow])
            P.do("pe", lambda e: e.matmul(PS[0:MAXR, 1024:1024 + PT],
                                          c_ones[:, 0:MAXR], idxrow[:, 0:PT],
                                          start=True, stop=True),
                 reads=[r_idxrow] + RC, writes=[r_bank[2]])
            msk = scr.bf(MSK, 512, p0=0, p1=MAXR)
            P.do("dve", lambda e, msk=msk: e.tensor_scalar(
                msk[:, 0:PT], PS[0:MAXR, 1024:1024 + PT], c_thr[:, :], None, ALU.is_ge),
                reads=[r_bank[2]] + RC, writes=[r_msk])
            P.do("dve", lambda e, th=th, msk=msk: e.tensor_mul(
                umsk[:, th * PT:(th + 1) * PT], PS[0:MAXR, 0:PT], msk[:, 0:PT]),
                reads=[r_bank[0], r_msk], writes=[r_umsk])

        last_g = (WIN + (NPASS - 1) * PT) // GRP
        INPLACE_T8 = (NSUB - QS) if (NOREBUILD and last_g % 2 == 1) else NSUB

        def z2pre_ap(t8, col0, ncol):
            if NOREBUILD:
                if t8 < INPLACE_T8:
                    return a4.bf(t8 * D + col0, ncol)
                return a1.bf((t8 - INPLACE_T8) * D + col0, ncol)
            if t8 < 4:
                return a1.bf(t8 * D + col0, ncol)
            if t8 < 7:
                return a4.bf((t8 - 3) * D + col0, ncol)
            return a2.bf(cB_off + 2 * D + col0, ncol)

        r_z2 = [Res() for _ in range(NSUB)]
        for t8 in range(NSUB):
            if NOREBUILD:
                if t8 < INPLACE_T8:
                    r_z2[t8].take(r_vtm)
                else:
                    r_z2[t8].take(r_xres)
            elif t8 < 4:
                r_z2[t8].take(r_D)
                r_z2[t8].take(r_xres)
                r_z2[t8].take(r_z1tm[0])
                r_z2[t8].take(r_z1tm[1])
            elif t8 < 7:
                r_z2[t8].take(r_vtm)
            else:
                r_z2[t8].take(r_cB)
        r_z2out2 = [r_z2out, Res()]
        r_z2out2[1].take(r_z2out)

        def emit_ln2(t8):
            zpre = z2pre_ap(t8, 0, D)
            rzo = r_z2out2[t8 % 2]
            if t8 < PT // 128:
                rzo.take(r_z1[0])
                zout = a5.f32((t8 % 2) * 2 * D, D)
            else:
                rzo.take(r_z1[1])
                zout = a3.f32((t8 % 2) * 2 * D, D)
            ln_chain(t8 % NSLOT, zpre, r_z2[t8], zout, rzo, skip_stats=True)
            P.do("sdma", lambda e, t8=t8, zout=zout: e.dma_start(
                out=y[t8 * 128:(t8 + 1) * 128, :], in_=zout),
                reads=[rzo], writes=[r_y[t8 % 2]])

        waves = [list(range(s, min(s + 4, NSUB))) for s in range(0, NSUB, 4)]
        ch_f = max(1, min(8, DC, SLC_N // 512))
        n_chf = DC // ch_f
        for wave in waves:
            for ob in range(OB):
                for ch in range(n_chf):
                    wch = scr.bf(SLC[ch % 2], ch_f * 512)
                    P.do("gdma", lambda e, wch=wch, ob=ob, ch=ch: e.dma_start(
                        out=wch[:, :].rearrange("p (s m) -> p s m", m=512),
                        in_=W_r[ob, ch * ch_f:(ch + 1) * ch_f].rearrange("s p m -> p s m"),
                        max_dma_last_dim=8192),
                        writes=[r_slc[ch % 2]])
                    for dcl in range(ch_f):
                        dc = ch * ch_f + dcl
                        fns = []
                        for tj, t8 in enumerate(wave):
                            fns.append(lambda e, tj=tj, t8=t8, dc=dc, wch=wch, dcl=dcl: e.matmul(
                                bank(tj), z1_ap(dc, t8 * 128, 128),
                                wch[:, dcl * 512:(dcl + 1) * 512],
                                start=(dc == 0), stop=False))
                        P.do("pe", fns, reads=[r_slc[ch % 2], r_z1[0], r_z1[1]],
                             writes=[r_bank[tj] for tj in range(len(wave))] if dc == 0 else [])
                fns = []
                for tj, t8 in enumerate(wave):
                    fns.append(lambda e, tj=tj, t8=t8, ob=ob: e.matmul(
                        bank(tj), umsk[:, t8 * 128:(t8 + 1) * 128],
                        cB[:, ob * 512:(ob + 1) * 512], start=False, stop=False))
                    fns.append(lambda e, tj=tj, ob=ob: e.matmul(
                        bank(tj), c_ones[:, 0:128], c_brow[:, ob * 512:(ob + 1) * 512],
                        start=False, stop=True))
                tk_g = P.do("pe", fns, reads=[r_umsk, r_cB, r_brow] + RC,
                            writes=[r_bank[tj] for tj in range(len(wave))])
                for tj, t8 in enumerate(wave):
                    noc = min(4, DC - ob * 4)
                    if NOREBUILD and t8 >= INPLACE_T8:
                        P.do("dve", lambda e, tj=tj, t8=t8, ob=ob, noc=noc: e.tensor_add(
                            z2pre_ap(t8, ob * 512, noc * 128),
                            PS[:, tj * 512: tj * 512 + noc * 128],
                            z2pre_ap(t8, ob * 512, noc * 128)),
                            reads=[r_bank[tj], r_xres], writes=[r_z2[t8]],
                            extra_waits=[tk_g])
                    else:
                        fns = []
                        for j in range(noc):
                            oc = ob * 4 + j
                            fns.append(lambda e, j=j, oc=oc, t8=t8: e.transpose(
                                bankb(4)[:, j * 128:(j + 1) * 128],
                                z1_ap(oc, t8 * 128, 128), c_id[:, :]))
                        P.do("pe", fns, reads=[r_z1[0], r_z1[1]] + RC, writes=[r_bank[4]],
                             extra_waits=[tk_g])
                        z1t = scr.bf(Z1T, 512)
                        P.do("dve", lambda e, z1t=z1t, noc=noc: e.tensor_copy(
                            z1t[:, 0:noc * 128], bankb(4)[:, 0:noc * 128]),
                            reads=[r_bank[4]], writes=[r_sc12])
                        P.do("dve", lambda e, tj=tj, t8=t8, ob=ob, z1t=z1t, noc=noc: e.tensor_add(
                            z2pre_ap(t8, ob * 512, noc * 128), PS[:, tj * 512: tj * 512 + noc * 128],
                            z1t[:, 0:noc * 128]),
                            reads=[r_bank[tj], r_sc12], writes=[r_z2[t8]],
                            extra_waits=[tk_g])
                    slot = t8 % NSLOT
                    P.do("dve", lambda e, t8=t8, slot=slot, ob=ob: e.bn_stats(
                        stats[:, slot * NST6 + ob * 6: slot * NST6 + (ob + 1) * 6],
                        z2pre_ap(t8, ob * 512, 512)),
                        reads=[r_z2[t8]], writes=[r_stats[slot]])
            if ob == OB - 1:
                for t8w in wave:
                    emit_ln2(t8w)

        for ry in r_y:
            ysem = P.semh[ry.dmasem]
            final_cnt = P.cnt[ry.dmasem]
            P.raw("sdma", lambda e, ysem=ysem, v=final_cnt: e.wait_ge(ysem, v))
        return _finish(nc, P, es)


def _finish(nc, P, es):
    if True:
        block = es.enter_context(nc.Block())

        @block.tensor
        def _(t):
            for fn in P.items["pe"]:
                fn(t)

        @block.scalar
        def _(sc):
            for fn in P.items["act"]:
                fn(sc)

        @block.vector
        def _(v):
            for fn in P.items["dve"]:
                fn(v)

        @block.gpsimd
        def _(g):
            for fn in P.items["gdma"]:
                fn(g)

        @block.sync
        def _(s):
            for fn in P.items["sdma"]:
                fn(s)
    return nc


def host_constants(cfg):
    c = derived(cfg)
    MAXR = c["MAXR"]
    out = {}
    out["idnt"] = np.eye(128, dtype=np.float32)
    pm = np.zeros((128, 128), dtype=np.float32)
    half = c["HD"] // 2
    for i in range(128):
        pm[i, (i + half) % c["HD"]] = 1.0
    out["p128"] = pm
    BIG = -30720.0
    i_ = np.arange(128)[:, None]
    j_ = np.arange(128)[None, :]
    out["m1t"] = np.where(j_ > i_, 0.0, BIG).astype(np.float32)
    out["m2t"] = np.where(j_ <= i_, 0.0, BIG).astype(np.float32)
    th = np.zeros((MAXR, 1), dtype=np.float32)
    for j in range(MAXR):
        k = 0
        while j >= cfg["RANKS"][k]:
            k += 1
        th[j, 0] = k - 0.5
    out["thr"] = th
    out["ones_r"] = np.ones((1, 512), dtype=np.float32)
    return out


def host_percore(cfg, core, x_full):
    c = derived(cfg)
    D, OWN, WIN, NSUB, HD = c["D"], c["OWN"], c["WIN"], c["NSUB"], c["HD"]
    cb, ci = core // c["NCORE_PER_B"], core % c["NCORE_PER_B"]
    t0 = ci * OWN
    xo = np.ascontiguousarray(x_full[cb, t0:t0 + OWN, :]).astype(np.float32)
    if ci == 0:
        xhalo = np.zeros((WIN, D), dtype=np.float32)
    else:
        xhalo = np.ascontiguousarray(x_full[cb, t0 - WIN:t0, :]).astype(np.float32)
    pos = np.arange(t0 - WIN, t0 + OWN, dtype=np.float64)
    pos = np.clip(pos, 0, None)
    half = HD // 2
    inv = cfg["THETA"] ** (-np.arange(half, dtype=np.float64) / half)
    ang = inv[:, None] * pos[None, :]
    cos = np.cos(ang)
    sin = np.sin(ang)
    cos_e = np.concatenate([cos, cos], axis=0).astype(np.float32)
    sin_e = np.concatenate([-sin, sin], axis=0).astype(np.float32)
    corr = np.zeros((128, NSUB), dtype=np.float32)
    if ci == 0:
        ir = np.arange(128)
        for sg in range(NSUB):
            L = WIN - 128 * sg
            tri = np.clip(np.minimum(128, L) - ir - 1, 0, None)
            mid = max(0, min(WIN, L) - 128)
            corr[:, sg] = -(tri + mid)
    return dict(xo=xo, xh=xhalo, cos_e=cos_e, sin_e=sin_e, corr_l=corr)


def host_weights(cfg, W):
    c = derived(cfg)
    D, DC, HQ, HKV, OB = c["D"], c["DC"], c["HQ"], c["HKV"], c["OB"]
    MAXR, NG = c["MAXR"], c["NG"]
    out = {}

    def lslice(w):
        K, M = w.shape
        r = w.reshape(K // 128, 128, M // 128, 128)
        return np.ascontiguousarray(r.transpose(2, 1, 0, 3)).reshape(
            M // 128, 128, (K // 128) * 128).astype(np.float32)

    out["wq_l"] = lslice(W["wq"])
    out["wk_l"] = lslice(W["wk"])
    out["wv_l"] = lslice(W["wv"])
    wo = W["wo"]
    r = wo.reshape(DC, 128, OB, 512)
    out["wo_r"] = np.ascontiguousarray(r.transpose(2, 0, 1, 3)).astype(np.float32)
    Wf = W["W"]
    r = Wf.reshape(OB, 512, DC, 128)
    out["W_r"] = np.ascontiguousarray(r.transpose(0, 2, 3, 1)).astype(np.float32)
    A = W["A"]
    r = A.reshape(MAXR, DC, 128)
    out["A_l"] = np.ascontiguousarray(r.transpose(2, 1, 0)).reshape(128, DC * MAXR).astype(np.float32)
    out["B_l"] = np.ascontiguousarray(W["B"].T).astype(np.float32)
    g = W["gate_w"].reshape(NG, DC, 128)
    out["g_l"] = np.ascontiguousarray(g.transpose(2, 1, 0)).reshape(128, DC * NG).astype(np.float32)
    out["gb_c"] = np.ascontiguousarray(W["gate_b"].reshape(NG, 1)).astype(np.float32)
    out["b_row"] = np.ascontiguousarray(W["b"].reshape(1, D)).astype(np.float32)
    return out


def make_in_maps(cfg, x, wdict):
    c = derived(cfg)
    consts = host_constants(cfg)
    wmats = host_weights(cfg, wdict)
    n_cores = c["B"] * c["NCORE_PER_B"]
    in_maps = []
    for core in range(n_cores):
        m = dict(consts)
        m.update(wmats)
        m.update(host_percore(cfg, core, x))
        in_maps.append(m)
    return in_maps


def kernel(L0_attn_x, L0_attn_wq, L0_attn_wk, L0_attn_wv, L0_attn_wo,
           L0_ffn_W, L0_ffn_b, L0_ffn_A, L0_ffn_B, L0_ffn_gate_w, L0_ffn_gate_b):
    cfg = REAL
    c = derived(cfg)
    x = np.asarray(L0_attn_x, dtype=np.float32)
    in_maps = make_in_maps(cfg, x, dict(
        wq=np.asarray(L0_attn_wq, np.float32), wk=np.asarray(L0_attn_wk, np.float32),
        wv=np.asarray(L0_attn_wv, np.float32), wo=np.asarray(L0_attn_wo, np.float32),
        W=np.asarray(L0_ffn_W, np.float32), b=np.asarray(L0_ffn_b, np.float32),
        A=np.asarray(L0_ffn_A, np.float32), B=np.asarray(L0_ffn_B, np.float32),
        gate_w=np.asarray(L0_ffn_gate_w, np.float32),
        gate_b=np.asarray(L0_ffn_gate_b, np.float32)))
    nc = build_program(cfg)
    res = run_bass_kernel_spmd(nc, in_maps, core_ids=list(range(8)))
    B, S, Dm = x.shape
    out = np.empty((B, S, Dm), dtype=np.float32)
    for core in range(8):
        cb, ci = core // c["NCORE_PER_B"], core % c["NCORE_PER_B"]
        out[cb, ci * c["OWN"]:(ci + 1) * c["OWN"], :] = res.results[core]["y"]
    return out
